# version 11
# baseline (speedup 1.0000x reference)
import numpy as np
import ml_dtypes
import concourse.bass as bass
import concourse.mybir as mybir
from concourse.bass_utils import run_bass_kernel_spmd

F32 = mybir.dt.float32
BF16 = mybir.dt.bfloat16
I8 = mybir.dt.int8
AF = mybir.ActivationFunctionType
ALU = mybir.AluOpType
AX = mybir.AxisListType

T = 1024
H = 2048
NH = 16
HD = 128
BLK = 256
NB = 4
INTER = 8192
NT = T // 128
NC = H // 128


class Builder:
    SEMS = {"g": ("gd", 16), "y": ("yd", 16)}

    def __init__(self):
        self.ops = []

    def add(self, eng, fn, reads=(), writes=(), dma=False, tag=""):
        self.ops.append([eng, fn, tuple(reads), tuple(writes), dma, tag])
        return len(self.ops) - 1

    def finalize(self, nc):
        ops = self.ops
        n = len(ops)
        deps = [set() for _ in range(n)]
        last_w = {}
        readers = {}
        never = {}
        for i, (eng, fn, rd, wr, dma, tag) in enumerate(ops):
            for k in rd:
                j = last_w.get(k)
                if j is not None:
                    deps[i].add(j)
                elif k not in never:
                    never[k] = tag
            for k in wr:
                j = last_w.get(k)
                if j is not None:
                    deps[i].add(j)
                for j2 in readers.get(k, ()):
                    deps[i].add(j2)
            for k in rd:
                readers.setdefault(k, []).append(i)
            for k in wr:
                last_w[k] = i
                readers[k] = []
        if never:
            import sys
            print(f"BUILDER WARNING: reads before any write: {never}",
                  file=sys.stderr)
        epos = [0] * n
        ecnt = {}
        for i, op in enumerate(ops):
            epos[i] = ecnt.get(op[0], 0)
            ecnt[op[0]] = epos[i] + 1
        selfwait = [set() for _ in range(n)]
        for i in range(n):
            if ops[i][0] != "v":
                continue
            for j in deps[i]:
                if ops[j][0] == "v" and not ops[j][4] and epos[i] - epos[j] == 1:
                    selfwait[i].add(j)
        for i in range(n):
            deps[i] = {j for j in deps[i]
                       if ops[j][0] != ops[i][0] or ops[j][4]}
        needs = [False] * n
        for i in range(n):
            for j in deps[i]:
                needs[j] = True
            for j in selfwait[i]:
                needs[j] = True
        sem_names = {"g": ("gd", 16), "y": ("yd", 16), "t": ("ts", 1),
                     "v": ("vs", 1), "s": ("ss", 1)}
        counters = {}
        val_after = [0] * n
        sem_of = [None] * n
        for i, (eng, fn, rd, wr, dma, tag) in enumerate(ops):
            if needs[i] or dma:
                name, amt = sem_names[eng]
                if dma:
                    amt = 16
                counters[name] = counters.get(name, 0) + amt
                val_after[i] = counters[name]
                sem_of[i] = (name, amt)
        waited = {}
        waits = [dict() for _ in range(n)]
        for i in range(n):
            eng = ops[i][0]
            for j in list(deps[i]) + list(selfwait[i]):
                if sem_of[j] is None:
                    raise RuntimeError(f"dep without inc: {ops[j][5]} <- {ops[i][5]}")
                name, _ = sem_of[j]
                v = val_after[j]
                if waits[i].get(name, -1) < v:
                    waits[i][name] = v
        self._emit(nc, ops, waits, sem_of, sem_names, counters)

    def _emit(self, nc, ops, waits, sem_of, sem_names, finals):
        from contextlib import ExitStack
        with ExitStack() as ctx:
            sems = {}
            for name, _ in sem_names.values():
                sems[name] = ctx.enter_context(nc.semaphore(name))
            block = ctx.enter_context(nc.Block())
            engines = ["g", "y", "t", "v", "s"]
            secs = {"g": block.gpsimd, "y": block.sync, "t": block.tensor,
                    "v": block.vector, "s": block.scalar}

            def make_body(eng):
                def body(E):
                    waited = {}
                    for i, (e2, fn, rd, wr, dma, tag) in enumerate(ops):
                        if e2 != eng:
                            continue
                        for name, v in sorted(waits[i].items()):
                            if waited.get(name, -1) >= v:
                                continue
                            E.wait_ge(sems[name], v)
                            waited[name] = v
                        ins = fn(E)
                        if sem_of[i] is not None:
                            if ins is None:
                                raise RuntimeError(f"op {tag} needs inc but returned None")
                            name, amt = sem_of[i]
                            ins.then_inc(sems[name], amt)
                    for name in sorted(finals):
                        if waited.get(name, -1) < finals[name]:
                            E.wait_ge(sems[name], finals[name])
                return body
            for eng in engines:
                secs[eng](make_body(eng))


def build_nc(debug_taps=None, phases=5):
    nc = bass.Bass("TRN2", target_bir_lowering=False)

    d_x = nc.dram_tensor("x_own", [T, H], F32, kind="ExternalInput")
    d_xp = nc.dram_tensor("x_peer", [T, H], F32, kind="ExternalInput")
    d_wq = nc.dram_tensor("wq", [128, NH, NC, 384], F32, kind="ExternalInput")
    d_wga = nc.dram_tensor("wga", [128, 4, NC, 512], F32, kind="ExternalInput")
    d_wo = nc.dram_tensor("wo", [128, NC, H], F32, kind="ExternalInput")
    d_wgf = nc.dram_tensor("wgf", [128, 64, NC, 128], I8, kind="ExternalInput")
    d_wuf = nc.dram_tensor("wuf", [128, 64, NC, 128], I8, kind="ExternalInput")
    d_wdf = nc.dram_tensor("wdf", [128, 4, 16, 4, 512], I8, kind="ExternalInput")
    d_slope = nc.dram_tensor("slope", [1, NH], F32, kind="ExternalInput")
    d_normw = nc.dram_tensor("normw", [128, NC], F32, kind="ExternalInput")
    d_sg = nc.dram_tensor("sgate", [128, 64], F32, kind="ExternalInput")
    d_su = nc.dram_tensor("sup", [128, 64], F32, kind="ExternalInput")
    d_sd = nc.dram_tensor("sdown", [1, H], F32, kind="ExternalInput")
    d_mask = nc.dram_tensor("mask", [1, 1], F32, kind="ExternalInput")
    d_ident = nc.dram_tensor("identc", [128, 128], BF16, kind="ExternalInput")
    d_qcol = nc.dram_tensor("qcolc", [128, 2], F32, kind="ExternalInput")
    d_kcol = nc.dram_tensor("kcolc", [128, 2], F32, kind="ExternalInput")
    d_mnc = nc.dram_tensor("mncc", [128, 512], F32, kind="ExternalInput")
    d_out = nc.dram_tensor("out", [T, H], F32, kind="ExternalOutput")
    taps_out = {}
    if debug_taps:
        for name, shape, dt in debug_taps:
            taps_out[name] = nc.dram_tensor("tap_" + name, shape, dt,
                                            kind="ExternalOutput")

    ident = nc.alloc_sbuf_tensor("ident", [128, 128], BF16)
    slope_b = nc.alloc_sbuf_tensor("slope_b", [128, NH], F32)
    negs = nc.alloc_sbuf_tensor("negs", [128, NH], F32)
    qd = nc.alloc_sbuf_tensor("qd", [128, 2 * NH], F32)
    kd = nc.alloc_sbuf_tensor("kd", [128, 2 * NH], F32)
    bd = nc.alloc_sbuf_tensor("bd", [128, NH], F32)
    qcol = nc.alloc_sbuf_tensor("qcol", [128, 2], F32)
    kcol = nc.alloc_sbuf_tensor("kcol", [128, 2], F32)
    riota = nc.alloc_sbuf_tensor("riota", [128, 1], F32)
    mask_c = nc.alloc_sbuf_tensor("mask_c", [128, 1], F32)
    norm_c = nc.alloc_sbuf_tensor("norm_c", [128, NC], F32)
    sg_c = nc.alloc_sbuf_tensor("sg_c", [128, 64], F32)
    su_c = nc.alloc_sbuf_tensor("su_c", [128, 64], F32)
    sdb = nc.alloc_sbuf_tensor("sdb", [128, H], BF16)
    ss_all = nc.alloc_sbuf_tensor("ss_all", [128, 128], F32)
    rstd_all = nc.alloc_sbuf_tensor("rstd_all", [128, NT], F32)
    lnst = nc.alloc_sbuf_tensor("lnst", [128, 24], F32)
    gate_sb = nc.alloc_sbuf_tensor("gate_sb", [128, 1024], BF16)
    sdt_b = nc.alloc_sbuf_tensor("sdt_b", [128, 512], BF16)
    utmp = nc.alloc_sbuf_tensor("utmp", [128, 512], BF16)
    dtmp32 = nc.alloc_sbuf_tensor("dtmp32", [128, 2 * NH], F32)
    dgin = nc.alloc_sbuf_tensor("dgin", [128, 512], BF16)

    AR_BYTES = 200704
    arena = nc.alloc_sbuf_tensor("arena", [128, AR_BYTES], mybir.dt.uint8)
    base = nc.lookup_mloc(arena).addr

    def at(name, shape, dt, off):
        return nc.alloc_sbuf_tensor_at(name, shape, dt, offset=base + off)

    xnT = at("xnT", [128, NC * 1024], BF16, 0)
    A2 = 32768
    xnTp = at("xnTp", [128, NC * 1024], BF16, A2)
    gatedT = at("gatedT", [128, NC * 1024], BF16, A2)
    A3 = 65536
    o_all = at("o_all", [128, NT * H], F32, A3)
    x0 = at("x0", [128, H], F32, A3)
    x1 = at("x1", [128, H], F32, A3 + 8192)
    A4 = 131072
    wq_s = [at("wq_s0", [128, NC * 384], BF16, A4),
            at("wq_s1", [128, NC * 384], BF16, A4 + 12288)]
    dumpLN = at("dumpLN", [128, H], BF16, A3 + 16384)
    xn0 = at("xn0", [128, H], BF16, A3 + 20480)
    xn1 = at("xn1", [128, H], BF16, A3 + 24576)
    kpe = at("kpe", [128, T], BF16, A4 + 24576)
    vpe = at("vpe", [128, T], BF16, A4 + 26624)
    q_f2 = [at("q_f0", [128, T], BF16, A4 + 28672),
            at("q_f1", [128, T], BF16, A4 + 30720)]
    k_t2 = [at("k_t0", [128, T], BF16, A4 + 32768),
            at("k_t1", [128, T], BF16, A4 + 34816)]
    v_t2 = [at("v_t0", [128, T], BF16, A4 + 36864),
            at("v_t1", [128, T], BF16, A4 + 38912)]
    kdk_p = at("kdk_p", [128, 256], BF16, A4 + 40960)
    kdk_s = at("kdk_s", [128, 256], BF16, A4 + 41472)
    k_f = at("k_f", [128, 256], BF16, A4 + 41984)
    KV2 = [at("KV0", [128, 128], F32, A4 + 42496),
           at("KV1", [128, 128], F32, A4 + 43008)]
    S_b2 = [at("S_b0", [128, 128], BF16, A4 + 43520),
            at("S_b1", [128, 128], BF16, A4 + 43776)]
    mn = at("mn", [128, 512], F32, A4 + 46080)
    gated_tok2 = [at("gated_tok", [128, H], BF16, A4 + 65536),
                  at("gated_tokB", [128, H], BF16, A4 + 44032)]
    wg_s = at("wg_s", [128, 4 * NC * 512], BF16, A4)
    wo_q = [at("wo_q0", [128, 4 * H], BF16, A4),
            at("wo_q1", [128, 4 * H], BF16, A4 + 16384)]
    fuse = at("fuse", [128, 16 * 1024], BF16, A4)
    wd_q = [at("wd_q0", [128, 16 * 512], BF16, A2),
            at("wd_q1", [128, 16 * 512], BF16, A2 + 16384),
            at("wd_q2", [128, 16 * 512], BF16, A4 + 32768)]
    A5 = 180224
    xres = [at("xres0", [128, H], F32, A5),
            at("xres1", [128, H], F32, A5 + 8192)]
    wgf_s = [at("wgf_s0", [128, H], BF16, A5),
             at("wgf_s1", [128, H], BF16, A5 + 8192)]
    wuf_s = [at("wuf_s0", [128, H], BF16, A5 + 4096),
             at("wuf_s1", [128, H], BF16, A5 + 12288)]
    ost = [at("ost0", [128, 512], F32, A5 + 16384),
           at("ost1", [128, 512], F32, A5 + 18432)]
    diag_all = at("diag_all", [128, NH * 512], BF16, A5)

    pb = [nc.alloc_psum_tensor(f"pb{i}", [128, 512], F32) for i in range(7)]
    pbt = nc.alloc_psum_tensor("pbt", [128, 1024], BF16)

    B = Builder()

    B.add("g", lambda E: E.dma_start(slope_b[:, :], d_slope.ap().broadcast_to((128, NH))),
          writes=["slope_b"], dma=True, tag="ld_slope")
    B.add("g", lambda E: E.dma_start(mask_c[:, :], d_mask.ap().broadcast_to((128, 1))),
          writes=["mask_c"], dma=True, tag="ld_mask")
    B.add("g", lambda E: E.dma_start(sdb[:, :], d_sd.ap().broadcast_to((128, H))),
          writes=["sdb"], dma=True, tag="ld_sdb")
    B.add("y", lambda E: E.dma_start(norm_c[:, :], d_normw[:, :]),
          writes=["norm_c"], dma=True, tag="ld_norm")
    B.add("y", lambda E: E.dma_start(sg_c[:, :], d_sg[:, :]),
          writes=["sg_c"], dma=True, tag="ld_sg")
    B.add("y", lambda E: E.dma_start(su_c[:, :], d_su[:, :]),
          writes=["su_c"], dma=True, tag="ld_su")

    B.add("y", lambda E: E.dma_start(qcol[:, :], d_qcol[:, :]),
          writes=["qcol"], dma=True, tag="ld_qcol")
    B.add("y", lambda E: E.dma_start(kcol[:, :], d_kcol[:, :]),
          writes=["kcol"], dma=True, tag="ld_kcol")
    B.add("y", lambda E: E.dma_start(ident[:, :], d_ident[:, :]),
          writes=["ident"], dma=True, tag="ld_ident")
    B.add("y", lambda E: E.dma_start(mn[:, :], d_mnc[:, :]),
          writes=["mn"], dma=True, tag="ld_mn")
    B.add("v", lambda E: E.tensor_scalar_mul(negs[:, :], slope_b[:, :], -1.0),
          reads=["slope_b"], writes=["negs"], tag="negs")
    for par in range(2):
        B.add("v", lambda E, par=par: E.tensor_scalar_mul(
            dtmp32[:, par * NH:(par + 1) * NH], slope_b[:, :],
            qcol[:, par:par + 1]),
            reads=["slope_b", "qcol"], writes=["dtq"], tag=f"dtq{par}")
    for par in range(2):
        B.add("s", lambda E, par=par: E.activation(
            qd[:, par * NH:(par + 1) * NH],
            dtmp32[:, par * NH:(par + 1) * NH], AF.Exp),
            reads=["dtq"], writes=["qd"], tag=f"qd{par}")
    for par in range(2):
        B.add("v", lambda E, par=par: E.tensor_scalar_mul(
            dtmp32[:, par * NH:(par + 1) * NH], slope_b[:, :],
            kcol[:, par:par + 1]),
            reads=["slope_b", "kcol", "qd"], writes=["dtk"], tag=f"dtk{par}")
    for par in range(2):
        B.add("s", lambda E, par=par: E.activation(
            kd[:, par * NH:(par + 1) * NH],
            dtmp32[:, par * NH:(par + 1) * NH], AF.Exp),
            reads=["dtk"], writes=["kd"], tag=f"kd{par}")
    B.add("s", lambda E: E.activation(bd[:, :], slope_b[:, :], AF.Exp,
                                      scale=-float(BLK)),
          reads=["slope_b"], writes=["bd"], tag="bd")

    def wq_load(h):
        slot = h % 2
        extra = []
        B.add("g", lambda E, h=h, slot=slot: E.dma_start(
            wq_s[slot][:, :].rearrange("p (c o) -> p c o", o=384),
            d_wq[:, h, :, :]),
            writes=[f"wq{slot}"] + extra, dma=True, tag=f"wq_ld{h}")

    def ln_pass(src_dram, dst_T, dst_key, pref, load_eng="g", tr_eng="y"):
        for tt in range(NT):
            s = tt % 2
            xsl = x0 if s == 0 else x1
            xk = f"x{s}"
            lk = lambda col: f"ln_{s}_{col}"
            B.add(load_eng, lambda E, tt=tt, xsl=xsl, src=src_dram:
                  E.dma_start(xsl[:, :], src[tt * 128:(tt + 1) * 128, :]),
                  writes=[xk], dma=True, tag=f"{pref}_x{tt}")
            B.add("s", lambda E, xsl=xsl, s=s:
                  E.activation(dumpLN[:, :], xsl[:, :], AF.Identity,
                               accum_out=lnst[:, 12 * s:12 * s + 1]),
                  reads=[xk], writes=[lk("sum"), "dumpLN"], tag=f"{pref}_sum{tt}")
            B.add("s", lambda E, xsl=xsl, s=s:
                  E.activation(dumpLN[:, :], xsl[:, :], AF.Square,
                               accum_out=lnst[:, 12 * s + 1:12 * s + 2]),
                  reads=[xk], writes=[lk("sq"), "dumpLN"], tag=f"{pref}_sq{tt}")
            B.add("v", lambda E, s=s:
                  E.tensor_scalar_mul(lnst[:, 12 * s + 2:12 * s + 3],
                                      lnst[:, 12 * s:12 * s + 1], 1.0 / H),
                  reads=[lk("sum")], writes=[lk("mean")], tag=f"{pref}_mean{tt}")
            B.add("v", lambda E, s=s:
                  E.tensor_scalar_mul(lnst[:, 12 * s + 3:12 * s + 4],
                                      lnst[:, 12 * s + 1:12 * s + 2], 1.0 / H),
                  reads=[lk("sq")], writes=[lk("e2")], tag=f"{pref}_e2{tt}")
            B.add("v", lambda E, s=s:
                  E.tensor_mul(lnst[:, 12 * s + 4:12 * s + 5],
                               lnst[:, 12 * s + 2:12 * s + 3],
                               lnst[:, 12 * s + 2:12 * s + 3]),
                  reads=[lk("mean")], writes=[lk("msq")], tag=f"{pref}_msq{tt}")
            B.add("v", lambda E, s=s:
                  E.tensor_sub(lnst[:, 12 * s + 5:12 * s + 6],
                               lnst[:, 12 * s + 3:12 * s + 4],
                               lnst[:, 12 * s + 4:12 * s + 5]),
                  reads=[lk("e2"), lk("msq")], writes=[lk("veps")],
                  tag=f"{pref}_var{tt}")
            B.add("v", lambda E, s=s:
                  E.tensor_scalar_add(lnst[:, 12 * s + 5:12 * s + 6],
                                      lnst[:, 12 * s + 5:12 * s + 6], 1e-5),
                  reads=[lk("veps")], writes=[lk("veps")], tag=f"{pref}_veps{tt}")
            B.add("s", lambda E, s=s:
                  E.activation(lnst[:, 12 * s + 6:12 * s + 7],
                               lnst[:, 12 * s + 5:12 * s + 6], AF.Ln),
                  reads=[lk("veps")], writes=[lk("lnv")], tag=f"{pref}_ln{tt}")
            B.add("s", lambda E, s=s:
                  E.activation(lnst[:, 12 * s + 7:12 * s + 8],
                               lnst[:, 12 * s + 6:12 * s + 7], AF.Exp,
                               scale=-0.5),
                  reads=[lk("lnv")], writes=[lk("rstd")], tag=f"{pref}_rstd{tt}")
            B.add("v", lambda E, s=s:
                  E.tensor_mul(lnst[:, 12 * s + 8:12 * s + 9],
                               lnst[:, 12 * s + 2:12 * s + 3],
                               lnst[:, 12 * s + 7:12 * s + 8]),
                  reads=[lk("mean"), lk("rstd")], writes=[lk("nb")],
                  tag=f"{pref}_nb0{tt}")
            B.add("v", lambda E, s=s:
                  E.tensor_scalar_mul(lnst[:, 12 * s + 8:12 * s + 9],
                                      lnst[:, 12 * s + 8:12 * s + 9], -1.0),
                  reads=[lk("nb")], writes=[lk("nb")], tag=f"{pref}_nb{tt}")
            xnsl = xn0 if s == 0 else xn1
            xnk = f"xn{s}"
            B.add("v", lambda E, xsl=xsl, xnsl=xnsl, s=s:
                  E.tensor_scalar(xnsl[:, :], xsl[:, :],
                                  lnst[:, 12 * s + 7:12 * s + 8],
                                  lnst[:, 12 * s + 8:12 * s + 9],
                                  ALU.mult, ALU.add),
                  reads=[xk, lk("rstd"), lk("nb")], writes=[xnk],
                  tag=f"{pref}_xn{tt}")
            B.add(tr_eng, lambda E, tt=tt, xnsl=xnsl, dst_T=dst_T:
                  E.dma_start(
                      dst_T[:, :].rearrange("p (c t) -> p c t", t=1024)[:, :, tt * 128:(tt + 1) * 128],
                      xnsl[:, :].rearrange("p (c f) -> p c f", f=128),
                      transpose=True),
                  reads=[xnk], writes=[f"{dst_key}_{tt}"], dma=True,
                  tag=f"{pref}_tr{tt}")

    if phases >= 1:
        ln_pass(d_x, xnT, "xnT", "lno")
        if phases >= 2:
            wq_load(0)
            wq_load(1)
            for h in range(NH):
                B.add("v", lambda E, h=h: E.tensor_scalar_mul(
                    dgin[:, :], mn[:, :], negs[:, h:h + 1]),
                    reads=["mn", "negs", "dgin_rd"], writes=["dgin"],
                    tag=f"dgin{h}")
                B.add("s", lambda E, h=h: E.activation(
                    diag_all[:, h * 512:(h + 1) * 512], dgin[:, :], AF.Exp),
                    reads=["dgin"], writes=["diag_all", "dgin_rd"],
                    tag=f"dexp{h}")
        ln_pass(d_xp, xnTp, "xnTp", "lnp", load_eng="y", tr_eng="y")

    def fill_chunks(h):
        p = h % 2
        wqk = f"wq{p}"
        W = wq_s[p]
        qk_, kk, vk = f"q_f{p}", f"k_t{p}", f"v_t{p}"
        chunks = []

        def q_chunk(td, h=h, p=p, W=W, wqk=wqk, qk_=qk_):
            pk = f"pb{td}"
            for c in range(NC):
                B.add("t", lambda E, c=c, td=td, W=W: E.matmul(
                    pb[td][:, :], W[:, c * 384:c * 384 + 128],
                    xnT[:, c * 1024 + td * 512:c * 1024 + (td + 1) * 512],
                    start=(c == 0), stop=(c == NC - 1)),
                    reads=[wqk] + [f"xnT_{t}" for t in range(4 * td, 4 * td + 4)],
                    writes=[pk], tag=f"qmm{h}_{td}_{c}")
            B.add("s", lambda E, td=td, p=p: E.activation(
                q_f2[p][:, td * 512:(td + 1) * 512], pb[td][:, :], AF.Silu),
                reads=[pk], writes=[qk_], tag=f"qsilu{h}_{td}")
        chunks.append(lambda: q_chunk(0))
        chunks.append(lambda: q_chunk(1))

        def kv_chunk(xsrc, xkey, kdst, vdst, kk2, vk2, tts, pref, h=h, W=W,
                     wqk=wqk):
            for tt in tts:
                pk = f"pb{tt % 2}"
                for c in range(NC):
                    B.add("t", lambda E, c=c, tt=tt, xsrc=xsrc, W=W: E.matmul(
                        pb[tt % 2][:, 0:256],
                        xsrc[:, c * 1024 + tt * 128:c * 1024 + (tt + 1) * 128],
                        W[:, c * 384 + 128:c * 384 + 384],
                        start=(c == 0), stop=(c == NC - 1)),
                        reads=[wqk, f"{xkey}_{tt}"], writes=[pk],
                        tag=f"kvmm{pref}{h}_{tt}_{c}")
                B.add("s", lambda E, tt=tt, kdst=kdst: E.activation(
                    kdst[:, tt * 128:(tt + 1) * 128],
                    pb[tt % 2][:, 0:128], AF.Silu),
                    reads=[pk], writes=[kk2], tag=f"ksilu{pref}{h}_{tt}")
                B.add("s", lambda E, tt=tt, vdst=vdst: E.activation(
                    vdst[:, tt * 128:(tt + 1) * 128],
                    pb[tt % 2][:, 128:256], AF.Silu),
                    reads=[pk], writes=[vk2], tag=f"vsilu{pref}{h}_{tt}")
        for tts in ((0, 1), (2, 3), (4, 5), (6, 7)):
            chunks.append(lambda tts=tts, p=p: kv_chunk(
                xnT, "xnT", k_t2[p], v_t2[p], f"k_t{p}", f"v_t{p}", tts, "own"))
        for tts in ((0, 1), (2, 3), (4, 5), (6, 7)):
            chunks.append(lambda tts=tts: kv_chunk(
                xnTp, "xnTp", kpe, vpe, "kpe", "vpe", tts, "pe"))

        def prefix_chunk(j, h=h, p=p):
            kvk = f"KV{p}"
            for nck in range(2):
                B.add("v", lambda E, j=j, nck=nck, h=h: E.tensor_scalar_mul(
                    kdk_p[:, nck * 128:(nck + 1) * 128],
                    kpe[:, (2 * j + nck) * 128:(2 * j + nck + 1) * 128],
                    kd[:, nck * NH + h:nck * NH + h + 1]),
                    reads=["kpe", "kd"], writes=["kdk_p"],
                    tag=f"pkdk{h}_{j}_{nck}")
            for nck in range(2):
                B.add("t", lambda E, j=j, nck=nck: E.matmul(
                    pb[3][:, 0:128], kdk_p[:, nck * 128:(nck + 1) * 128],
                    vpe[:, (2 * j + nck) * 128:(2 * j + nck + 1) * 128],
                    start=(nck == 0), stop=(nck == 1)),
                    reads=["kdk_p", "vpe"], writes=["pb3"],
                    tag=f"pC{h}_{j}_{nck}")
            if j == 0:
                B.add("v", lambda E, p=p: E.tensor_copy(KV2[p][:, :],
                                                        pb[3][:, 0:128]),
                      reads=["pb3"], writes=[kvk], tag=f"pKV{h}_0")
            else:
                B.add("v", lambda E, h=h, p=p: E.scalar_tensor_tensor(
                    KV2[p][:, :], KV2[p][:, :], bd[:, h:h + 1],
                    pb[3][:, 0:128], ALU.mult, ALU.add),
                    reads=["pb3", kvk, "bd"], writes=[kvk],
                    tag=f"pKV{h}_{j}")
            if j == NB - 1:
                B.add("v", lambda E, p=p: E.tensor_scalar_mul(
                    KV2[p][:, :], KV2[p][:, :], mask_c[:, 0:1]),
                    reads=[kvk, "mask_c"], writes=[kvk], tag=f"maskKV{h}")
                B.add("v", lambda E, p=p: E.tensor_copy(S_b2[p][:, :],
                                                        KV2[p][:, :]),
                      reads=[kvk], writes=[f"S_b{p}"], tag=f"Sb{h}_init")
        for j in range(NB):
            chunks.append(lambda j=j: prefix_chunk(j))
        return chunks

    def scan_block_segs(h, j):
        p = h % 2
        qk_, kk, vk = f"q_f{p}", f"k_t{p}", f"v_t{p}"
        kvk, sbk, dk = f"KV{p}", f"S_b{p}", "diag_all"

        def seg1(h=h, j=j, p=p):
            for nck in range(2):
                B.add("t", lambda E, j=j, nck=nck, p=p: E.transpose(
                    pbt[:, nck * 128:(nck + 1) * 128],
                    k_t2[p][:, (2 * j + nck) * 128:(2 * j + nck + 1) * 128],
                    ident[:, :]),
                    reads=[kk, "ident"], writes=["pbt"],
                    tag=f"ktr{h}_{j}_{nck}")
            B.add("v", lambda E: E.tensor_copy(k_f[:, :], pbt[:, 0:256]),
                  reads=["pbt"], writes=["k_f"], tag=f"kf{h}_{j}")

        def seg2(h=h, j=j, p=p):
            for nck in range(2):
                B.add("t", lambda E, j=j, nck=nck, p=p: E.matmul(
                    pb[2][:, nck * 256:(nck + 1) * 256],
                    k_f[:, nck * 128:(nck + 1) * 128],
                    q_f2[p][:, j * 256:(j + 1) * 256],
                    start=True, stop=True),
                    reads=["k_f", qk_], writes=["pb2"],
                    tag=f"st{h}_{j}_{nck}")
            B.add("v", lambda E, h=h: E.tensor_mul(
                sdt_b[:, :], pb[2][:, :], diag_all[:, h * 512:(h + 1) * 512]),
                reads=["pb2", dk], writes=["sdt"], tag=f"sdt{h}_{j}")

        def seg3(h=h, j=j, p=p):
            for mc in range(2):
                bank = 4 + mc
                for nck in range(2):
                    B.add("t", lambda E, j=j, mc=mc, nck=nck, bank=bank, p=p:
                          E.matmul(
                              pb[bank][:, 0:128],
                              sdt_b[:, nck * 256 + mc * 128:nck * 256 + (mc + 1) * 128],
                              v_t2[p][:, (2 * j + nck) * 128:(2 * j + nck + 1) * 128],
                              start=(nck == 0), stop=(nck == 1)),
                          reads=["sdt", vk], writes=[f"pb{bank}"],
                          tag=f"op{h}_{j}_{mc}_{nck}")
            for mc in range(2):
                B.add("t", lambda E, j=j, mc=mc, p=p: E.matmul(
                    pb[6][:, mc * 128:(mc + 1) * 128],
                    q_f2[p][:, j * 256 + mc * 128:j * 256 + (mc + 1) * 128],
                    S_b2[p][:, :], start=True, stop=True),
                    reads=[qk_, sbk], writes=["pb6"],
                    tag=f"oi{h}_{j}_{mc}")
            for mc in range(2):
                tt = 2 * j + mc
                col = tt * H + h * 128
                B.add("v", lambda E, col=col, mc=mc, h=h: E.tensor_scalar_mul(
                    o_all[:, col:col + 128], pb[6][:, mc * 128:(mc + 1) * 128],
                    qd[:, mc * NH + h:mc * NH + h + 1]),
                    reads=["pb6", "qd"],
                    writes=(["o_all", "x0", "x1", "dumpLN", "xn0", "xn1"]
                            if (h == 0 and j == 0 and mc == 0) else ["o_all"]),
                    tag=f"evI{h}_{j}_{mc}")
                B.add("v", lambda E, col=col, mc=mc: E.tensor_add(
                    o_all[:, col:col + 128], o_all[:, col:col + 128],
                    pb[4 + mc][:, 0:128]),
                    reads=[f"pb{4 + mc}", "o_all"], writes=["o_all"],
                    tag=f"evP{h}_{j}_{mc}")
                B.add("s", lambda E, col=col, tt=tt, h=h: E.activation(
                    gate_sb[:, 512:640], o_all[:, col:col + 128], AF.Square,
                    accum_out=ss_all[:, tt * NH + h:tt * NH + h + 1]),
                    reads=["o_all"], writes=["ssdump", "ss_all"],
                    tag=f"ssq{h}_{j}_{mc}")

        def seg4(h=h, j=j, p=p):
            if j == NB - 1:
                return
            for nck in range(2):
                B.add("v", lambda E, j=j, nck=nck, h=h, p=p: E.tensor_scalar_mul(
                    kdk_s[:, nck * 128:(nck + 1) * 128],
                    k_t2[p][:, (2 * j + nck) * 128:(2 * j + nck + 1) * 128],
                    kd[:, nck * NH + h:nck * NH + h + 1]),
                    reads=[kk, "kd"], writes=["kdk_s"],
                    tag=f"kdk{h}_{j}_{nck}")
            for nck in range(2):
                B.add("t", lambda E, j=j, nck=nck, p=p: E.matmul(
                    pb[3][:, 0:128], kdk_s[:, nck * 128:(nck + 1) * 128],
                    v_t2[p][:, (2 * j + nck) * 128:(2 * j + nck + 1) * 128],
                    start=(nck == 0), stop=(nck == 1)),
                    reads=["kdk_s", vk], writes=["pb3"],
                    tag=f"C{h}_{j}_{nck}")
            B.add("v", lambda E, h=h, p=p: E.scalar_tensor_tensor(
                KV2[p][:, :], KV2[p][:, :], bd[:, h:h + 1], pb[3][:, 0:128],
                ALU.mult, ALU.add),
                reads=["pb3", kvk, "bd"], writes=[kvk],
                tag=f"KV{h}_{j}")
            B.add("v", lambda E, p=p: E.tensor_copy(S_b2[p][:, :],
                                                    KV2[p][:, :]),
                  reads=[kvk], writes=[sbk], tag=f"Sb{h}_{j}")
        return [seg1, seg2, seg3, seg4]

    if phases >= 2:
        for c in fill_chunks(0):
            c()
        for c in fill_chunks(1):
            c()
        for h in range(NH):
            if h + 2 < NH:
                wq_load(h + 2)
            pending = fill_chunks(h + 2) if h + 2 < NH else []
            segs = [s for j in range(NB) for s in scan_block_segs(h, j)]
            nslots = len(segs)
            k = 0
            for si, seg in enumerate(segs):
                seg()
                want = ((si + 1) * len(pending)) // nslots
                while k < want:
                    pending[k]()
                    k += 1
            while k < len(pending):
                pending[k]()
                k += 1

    if phases >= 3:
        for tt in range(NT):
            B.add("s", lambda E, tt=tt: E.activation(
                gate_sb[:, 0:NH], ss_all[:, tt * NH:(tt + 1) * NH], AF.Identity,
                accum_out=lnst[:, 0:1]),
                reads=["ss_all", "gdump"], writes=["r3sum", "gdump"], tag=f"r3s{tt}")
            B.add("v", lambda E: E.tensor_scalar(lnst[:, 2:3], lnst[:, 0:1],
                                                 1.0 / H, 1e-6, ALU.mult, ALU.add),
                  reads=["r3sum"], writes=["r3me"], tag=f"r3m{tt}")
            B.add("s", lambda E: E.activation(lnst[:, 1:2], lnst[:, 2:3], AF.Ln),
                  reads=["r3me"], writes=["r3ln"], tag=f"r3l{tt}")
            B.add("s", lambda E, tt=tt: E.activation(
                rstd_all[:, tt:tt + 1], lnst[:, 1:2], AF.Exp, scale=-0.5),
                reads=["r3ln"], writes=["rstd_all"], tag=f"r3e{tt}")
        B.add("g", lambda E: E.dma_start(
            wg_s[:, 0:14336].rearrange("p (a o) -> p a o", o=512),
            d_wga.ap().rearrange("p a c o -> p (a c) o")[:, 0:28, :]),
            writes=["wg_A", "wq0", "wq1", "kpe", "vpe"],
            dma=True, tag="wg_ldA")
        B.add("g", lambda E: E.dma_start(
            wg_s[:, 14336:].rearrange("p (a o) -> p a o", o=512),
            d_wga.ap().rearrange("p a c o -> p (a c) o")[:, 28:, :]),
            writes=["wg_B", "q_f0", "q_f1", "diag_all"],
            dma=True, tag="wg_ldB")
        for tt in range(NT):
            for ot in range(4):
                pk = f"pb{ot % 2}"
                for c in range(NC):
                    B.add("t", lambda E, tt=tt, ot=ot, c=c: E.matmul(
                        pb[ot % 2][:, :],
                        xnT[:, c * 1024 + tt * 128:c * 1024 + (tt + 1) * 128],
                        wg_s[:, (ot * NC + c) * 512:(ot * NC + c + 1) * 512],
                        start=(c == 0), stop=(c == NC - 1)),
                        reads=[f"xnT_{tt}",
                               "wg_A" if (ot * NC + c) < 28 else "wg_B"],
                        writes=[pk], tag=f"gmm{tt}_{ot}_{c}")
                gsl = ot % 2
                B.add("s", lambda E, ot=ot, gsl=gsl: E.activation(
                    gate_sb[:, gsl * 512:(gsl + 1) * 512], pb[ot % 2][:, :],
                    AF.Sigmoid),
                    reads=[pk], writes=[f"gate{gsl}"], tag=f"gsig{tt}_{ot}")
                tsl = tt % 2
                gtk = f"gated_tok{tsl}"
                extra = (["k_t0", "k_t1"] if (tt == 0 and ot == 0) else
                         (["mn"] if (tt == 1 and ot == 0) else []))
                B.add("v", lambda E, tt=tt, ot=ot, gsl=gsl, tsl=tsl:
                      E.scalar_tensor_tensor(
                          gated_tok2[tsl][:, ot * 512:(ot + 1) * 512],
                          o_all[:, tt * H + ot * 512:tt * H + (ot + 1) * 512],
                          rstd_all[:, tt:tt + 1],
                          gate_sb[:, gsl * 512:(gsl + 1) * 512],
                          ALU.mult, ALU.mult),
                      reads=["o_all", "rstd_all", f"gate{gsl}"],
                      writes=[gtk] + extra, tag=f"gated{tt}_{ot}")
            extra2 = ["xnTp"] if tt == 0 else []
            B.add("y", lambda E, tt=tt, tsl=tt % 2: E.dma_start(
                gatedT[:, :].rearrange("p (c t) -> p c t", t=1024)[:, :, tt * 128:(tt + 1) * 128],
                gated_tok2[tsl][:, :].rearrange("p (c f) -> p c f", f=128),
                transpose=True),
                reads=[f"gated_tok{tt % 2}"], writes=["gatedT"] + extra2,
                dma=True, tag=f"gtr{tt}")

    if phases >= 4:
        def wo_fetch(qo):
            slot = qo % 2
            extra = (["wg_A", "wg_B"] if qo in (0, 1) else [])
            wok = f"wo_q{slot}"
            B.add("g", lambda E, qo=qo, slot=slot: E.dma_start(
                wo_q[slot][:, :].rearrange("p (a f) -> p a f", f=H),
                d_wo[:, qo * 4:(qo + 1) * 4, :]),
                writes=[wok] + extra, dma=True, tag=f"wo_ld{qo}")
            for l in range(4):
                B.add("g", lambda E, qo=qo, slot=slot, l=l: E.tensor_scalar_mul(
                    wo_q[slot][:, l * H:(l + 1) * H],
                    wo_q[slot][:, l * H:(l + 1) * H],
                    norm_c[:, qo * 4 + l:qo * 4 + l + 1]),
                    reads=[wok, "norm_c"], writes=[wok], tag=f"wosc{qo}_{l}")

        wo_fetch(0)
        for qo in range(4):
            slot = qo % 2
            wok = f"wo_q{slot}"
            if qo + 1 < 4:
                wo_fetch(qo + 1)
            for tt in range(NT):
                if qo == 0:
                    exx = ["wg_A", "wg_B"] if tt in (0, 1) else []
                    B.add("y", lambda E, tt=tt: E.dma_start(
                        xres[tt % 2][:, :], d_x[tt * 128:(tt + 1) * 128, :]),
                        writes=[f"xres{tt % 2}"] + exx, dma=True, tag=f"xres{tt}")
                for ft in range(4):
                    pk = f"pb{ft % 2}"
                    for l in range(4):
                        B.add("t", lambda E, qo=qo, slot=slot, tt=tt, ft=ft, l=l:
                              E.matmul(
                                  pb[ft % 2][:, :],
                                  gatedT[:, (qo * 4 + l) * 1024 + tt * 128:(qo * 4 + l) * 1024 + (tt + 1) * 128],
                                  wo_q[slot][:, l * H + ft * 512:l * H + (ft + 1) * 512],
                                  start=(l == 0), stop=(l == 3)),
                              reads=["gatedT", wok], writes=[pk],
                              tag=f"omm{qo}_{tt}_{ft}_{l}")
                    col = tt * H + ft * 512
                    if qo == 0:
                        B.add("v", lambda E, col=col, ft=ft, tt=tt: E.tensor_add(
                            o_all[:, col:col + 512], xres[tt % 2][:, ft * 512:(ft + 1) * 512],
                            pb[ft % 2][:, :]),
                            reads=[pk, f"xres{tt % 2}"], writes=["o_all"],
                            tag=f"acc0_{tt}_{ft}")
                    else:
                        B.add("v", lambda E, col=col, ft=ft: E.tensor_add(
                            o_all[:, col:col + 512], o_all[:, col:col + 512],
                            pb[ft % 2][:, :]),
                            reads=[pk, "o_all"], writes=["o_all"],
                            tag=f"acc{qo}_{tt}_{ft}")

    if phases >= 5:
        st_ct = [0]

        def wd_fetch(u):
            qq, hh = divmod(u, 4)
            wsl = u % 3
            wdk = f"wd_q{wsl}"
            extra = (["gatedT"] if u in (0, 1) else
                     (["k_t0", "k_t1", "v_t0", "v_t1", "kdk_p", "kdk_s",
                       "k_f", "KV0", "KV1", "S_b0", "S_b1", "gated_tok1",
                       "mn"] if u == 2 else []))
            B.add("g", lambda E, qq=qq, hh=hh, wsl=wsl: E.dma_start(
                wd_q[wsl][:, :].rearrange("p (c i) -> p c i", i=512),
                d_wdf[:, qq, :, hh, :]),
                writes=[wdk] + [f"{wdk}_{ic}" for ic in range(16)] + extra,
                dma=True, tag=f"wd{qq}_{hh}")
            for ic in range(16):
                B.add("g", lambda E, ic=ic, hh=hh, wsl=wsl: E.tensor_mul(
                    wd_q[wsl][:, ic * 512:(ic + 1) * 512],
                    wd_q[wsl][:, ic * 512:(ic + 1) * 512],
                    sdb[:, hh * 512:(hh + 1) * 512]),
                    reads=[wdk, "sdb"], writes=[f"{wdk}_{ic}"],
                    tag=f"wdsc{qq}_{hh}_{ic}")

        for q in range(4):
            for ic in range(16):
                gi = q * 16 + ic
                sl = gi % 2
                extra = ([f"xres{sl}", "wg_A", "wg_B"] if gi in (0, 1) else [])
                B.add("g", lambda E, gi=gi, sl=sl: E.dma_start(
                    wgf_s[sl][:, :].rearrange("p (c i) -> p c i", i=128),
                    d_wgf[:, gi, :, :]),
                    writes=[f"wgf{sl}"] + extra, dma=True, tag=f"wgf{gi}")
                B.add("g", lambda E, gi=gi, sl=sl: E.dma_start(
                    wuf_s[sl][:, :].rearrange("p (c i) -> p c i", i=128),
                    d_wuf[:, gi, :, :]),
                    writes=[f"wuf{sl}"] + extra, dma=True, tag=f"wuf{gi}")
                for td in range(2):
                    gk, uk = f"pb{td}", f"pb{4 + td}"
                    for c in range(NC):
                        B.add("t", lambda E, sl=sl, c=c, td=td: E.matmul(
                            pb[td][:, :], wgf_s[sl][:, c * 128:(c + 1) * 128],
                            xnT[:, c * 1024 + td * 512:c * 1024 + (td + 1) * 512],
                            start=(c == 0), stop=(c == NC - 1)),
                            reads=[f"wgf{sl}"] + [f"xnT_{t}" for t in range(4 * td, 4 * td + 4)],
                            writes=[gk], tag=f"fg{gi}_{td}_{c}")
                    for c in range(NC):
                        B.add("t", lambda E, sl=sl, c=c, td=td: E.matmul(
                            pb[4 + td][:, :], wuf_s[sl][:, c * 128:(c + 1) * 128],
                            xnT[:, c * 1024 + td * 512:c * 1024 + (td + 1) * 512],
                            start=(c == 0), stop=(c == NC - 1)),
                            reads=[f"wuf{sl}"] + [f"xnT_{t}" for t in range(4 * td, 4 * td + 4)],
                            writes=[uk], tag=f"fu{gi}_{td}_{c}")
                    fcol = ic * 1024 + td * 512
                    extra2 = (["wo_q0", "wo_q1"] if gi == 0 and td == 0 else [])
                    B.add("s", lambda E, fcol=fcol, gi=gi, td=td: E.activation(
                        fuse[:, fcol:fcol + 512], pb[td][:, :], AF.Silu,
                        scale=sg_c[:, gi:gi + 1]),
                        reads=[gk, "sg_c"], writes=["fuse"] + extra2,
                        tag=f"fsilu{gi}_{td}")
                    B.add("s", lambda E, gi=gi, td=td: E.activation(
                        utmp[:, :], pb[4 + td][:, :],
                        AF.Identity, scale=su_c[:, gi:gi + 1]),
                        reads=[uk, "su_c"], writes=["utmp"], tag=f"fup{gi}_{td}")
                    B.add("v", lambda E, fcol=fcol, td=td: E.tensor_mul(
                        fuse[:, fcol:fcol + 512], fuse[:, fcol:fcol + 512],
                        utmp[:, :]),
                        reads=["fuse", "utmp"], writes=["fuse"],
                        tag=f"fmul{gi}_{td}")
            for ht in range(4):
                u = q * 4 + ht
                wsl = u % 3
                wdk = f"wd_q{wsl}"
                if ht == 0 and q == 0:
                    wd_fetch(0)
                    wd_fetch(1)
                if u + 2 < 16:
                    wd_fetch(u + 2)
                for tt in range(NT):
                    pk = f"pb{2 + tt % 2}"
                    for ic in range(16):
                        B.add("t", lambda E, ic=ic, tt=tt, ht=ht, wsl=wsl: E.matmul(
                            pb[2 + tt % 2][:, :],
                            fuse[:, ic * 1024 + tt * 128:ic * 1024 + (tt + 1) * 128],
                            wd_q[wsl][:, ic * 512:(ic + 1) * 512],
                            start=(ic == 0), stop=(ic == 15)),
                            reads=["fuse", f"{wdk}_{ic}"], writes=[pk],
                            tag=f"dmm{q}_{ht}_{tt}_{ic}")
                    col = tt * H + ht * 512
                    if q < 3:
                        B.add("v", lambda E, col=col, tt=tt: E.tensor_add(
                            o_all[:, col:col + 512], o_all[:, col:col + 512],
                            pb[2 + tt % 2][:, :]),
                            reads=[pk, "o_all"], writes=["o_all"],
                            tag=f"dacc{q}_{ht}_{tt}")
                    else:
                        osl = st_ct[0] % 2
                        exo = ["gated_tok0"] if st_ct[0] in (0, 1) else []
                        B.add("v", lambda E, col=col, tt=tt, osl=osl: E.tensor_add(
                            ost[osl][:, :], o_all[:, col:col + 512],
                            pb[2 + tt % 2][:, :]),
                            reads=[pk, "o_all"], writes=[f"ost{osl}"] + exo,
                            tag=f"fin{ht}_{tt}")
                        B.add("y", lambda E, tt=tt, ht=ht, osl=osl: E.dma_start(
                            d_out[tt * 128:(tt + 1) * 128, ht * 512:(ht + 1) * 512],
                            ost[osl][:, :]),
                            reads=[f"ost{osl}"], writes=["OUT"], dma=True,
                            tag=f"st{ht}_{tt}")
                        st_ct[0] += 1

    if debug_taps:
        ctx = dict(xnT=xnT, xnTp=xnTp, o_all=o_all, q_f=q_f, k_tok=k_tok, lnst=lnst,
                   v_tok=v_tok, kpe=kpe, vpe=vpe, KV=KV, S_b=S_b, qd=qd,
                   kd=kd, bd=bd, diag=diag, ss_all=ss_all, rstd_all=rstd_all,
                   gatedT=gatedT, fuse=fuse, sdb=sdb, gate_sb=gate_sb)
        for name, shape, dt in debug_taps:
            src, rkeys = TAP_SPECS[name](ctx)
            B.add("y", lambda E, src=src, name=name: E.dma_start(
                taps_out[name].ap(), src), reads=rkeys, writes=["TAPS"],
                dma=True, tag=f"tap_{name}")

    B.add("v", lambda E: E.engine_nop(), reads=["OUT", "TAPS"] if debug_taps
          else ["OUT"], tag="final_wait")

    B.finalize(nc)
    return nc


TAP_SPECS = {}


def _layout_inputs(L0_attn_hidden_states, L0_attn_slope_rate, L0_attn_w_qkv,
                   L0_attn_w_gate, L0_attn_w_out, L0_attn_norm_w,
                   L0_ffn_w_gate, L0_ffn_w_up, L0_ffn_w_down,
                   L0_ffn_s_gate, L0_ffn_s_up, L0_ffn_s_down):
    c = np.ascontiguousarray
    x = np.asarray(L0_attn_hidden_states, dtype=np.float32)
    wq = c(np.asarray(L0_attn_w_qkv, np.float32).reshape(16, 384, 16, 128)
           .transpose(3, 0, 2, 1))
    wga = c(np.asarray(L0_attn_w_gate, np.float32).reshape(4, 512, 16, 128)
            .transpose(3, 0, 2, 1))
    wo = c(np.asarray(L0_attn_w_out, np.float32).T.reshape(16, 128, 2048)
           .transpose(1, 0, 2))
    wgf = c(np.asarray(L0_ffn_w_gate, np.int8).reshape(16, 128, 64, 128)
            .transpose(1, 2, 0, 3))
    wuf = c(np.asarray(L0_ffn_w_up, np.int8).reshape(16, 128, 64, 128)
            .transpose(1, 2, 0, 3))
    wdf = c(np.asarray(L0_ffn_w_down, np.int8).reshape(4, 16, 128, 4, 512)
            .transpose(2, 0, 1, 3, 4))
    slope = c(np.asarray(L0_attn_slope_rate, np.float32).reshape(1, 16))
    normw = c(np.asarray(L0_attn_norm_w, np.float32).reshape(16, 128).T)
    sg = c(np.asarray(L0_ffn_s_gate, np.float32).reshape(64, 128).T)
    su = c(np.asarray(L0_ffn_s_up, np.float32).reshape(64, 128).T)
    sd = c(np.asarray(L0_ffn_s_down, np.float32).reshape(1, 2048))
    p = np.arange(128, dtype=np.float32)
    qcolc = np.stack([-(p + 1), -(p + 129)], axis=1).astype(np.float32)
    kcolc = np.stack([p - 255, p - 127], axis=1).astype(np.float32)
    identc = np.eye(128, dtype=np.float32).astype(ml_dtypes.bfloat16)
    mm = np.arange(256, dtype=np.float32)
    mnv = np.concatenate([mm[None, :] - p[:, None],
                          mm[None, :] - 128.0 - p[:, None]], axis=1)
    masktc = (mnv >= 0).astype(np.float32)
    mncc = np.maximum(mnv, 0.0).astype(np.float32)
    shared = dict(wq=wq, wga=wga, wo=wo, wgf=wgf, wuf=wuf, wdf=wdf,
                  slope=slope, normw=normw, sgate=sg, sup=su, sdown=sd,
                  identc=identc, qcolc=qcolc, kcolc=kcolc, mncc=mncc,
                  masktc=masktc)
    in_maps = []
    for core in range(8):
        seq, half = core // 2, core % 2
        m = dict(shared)
        m["x_own"] = c(x[seq, half * T:(half + 1) * T, :])
        m["x_peer"] = c(x[seq, (1 - half) * T:(2 - half) * T, :])
        m["mask"] = np.array([[float(half)]], dtype=np.float32)
        in_maps.append(m)
    return in_maps


def kernel(**inputs):
    in_maps = _layout_inputs(**inputs)
    nc = build_nc()
    res = run_bass_kernel_spmd(nc, in_maps, core_ids=list(range(8)))
    out = np.empty((4, 2048, 2048), dtype=np.float32)
    for core in range(8):
        seq, half = core // 2, core % 2
        out[seq, half * T:(half + 1) * T, :] = res.results[core]["out"]
    return out


# version 14
# speedup vs baseline: 1.1613x; 1.1613x over previous
import numpy as np
import ml_dtypes
import concourse.bass as bass
import concourse.mybir as mybir
from concourse.bass_utils import run_bass_kernel_spmd

F32 = mybir.dt.float32
BF16 = mybir.dt.bfloat16
I8 = mybir.dt.int8
AF = mybir.ActivationFunctionType
ALU = mybir.AluOpType
AX = mybir.AxisListType

T = 1024
H = 2048
NH = 16
HD = 128
BLK = 256
NB = 4
INTER = 8192
NT = T // 128
NC = H // 128


class Builder:
    SEMS = {"g": ("gd", 16), "y": ("yd", 16)}

    def __init__(self):
        self.ops = []

    def add(self, eng, fn, reads=(), writes=(), dma=False, tag=""):
        self.ops.append([eng, fn, tuple(reads), tuple(writes), dma, tag])
        return len(self.ops) - 1

    def finalize(self, nc):
        ops = self.ops
        n = len(ops)
        deps = [set() for _ in range(n)]
        last_w = {}
        readers = {}
        never = {}
        for i, (eng, fn, rd, wr, dma, tag) in enumerate(ops):
            for k in rd:
                j = last_w.get(k)
                if j is not None:
                    deps[i].add(j)
                elif k not in never:
                    never[k] = tag
            for k in wr:
                j = last_w.get(k)
                if j is not None:
                    deps[i].add(j)
                for j2 in readers.get(k, ()):
                    deps[i].add(j2)
            for k in rd:
                readers.setdefault(k, []).append(i)
            for k in wr:
                last_w[k] = i
                readers[k] = []
        if never:
            import sys
            print(f"BUILDER WARNING: reads before any write: {never}",
                  file=sys.stderr)
        epos = [0] * n
        ecnt = {}
        for i, op in enumerate(ops):
            epos[i] = ecnt.get(op[0], 0)
            ecnt[op[0]] = epos[i] + 1
        selfwait = [set() for _ in range(n)]
        for i in range(n):
            if ops[i][0] != "v":
                continue
            for j in deps[i]:
                if ops[j][0] == "v" and not ops[j][4] and epos[i] - epos[j] == 1:
                    selfwait[i].add(j)
        for i in range(n):
            deps[i] = {j for j in deps[i]
                       if ops[j][0] != ops[i][0] or ops[j][4]}
        needs = [False] * n
        for i in range(n):
            for j in deps[i]:
                needs[j] = True
            for j in selfwait[i]:
                needs[j] = True
        sem_names = {"g": ("gd", 16), "y": ("yd", 16), "t": ("ts", 1),
                     "v": ("vs", 1), "s": ("ss", 1)}
        counters = {}
        val_after = [0] * n
        sem_of = [None] * n
        for i, (eng, fn, rd, wr, dma, tag) in enumerate(ops):
            if needs[i] or dma:
                name, amt = sem_names[eng]
                if dma:
                    amt = 16
                counters[name] = counters.get(name, 0) + amt
                val_after[i] = counters[name]
                sem_of[i] = (name, amt)
        waited = {}
        waits = [dict() for _ in range(n)]
        for i in range(n):
            eng = ops[i][0]
            for j in list(deps[i]) + list(selfwait[i]):
                if sem_of[j] is None:
                    raise RuntimeError(f"dep without inc: {ops[j][5]} <- {ops[i][5]}")
                name, _ = sem_of[j]
                v = val_after[j]
                if waits[i].get(name, -1) < v:
                    waits[i][name] = v
        self._emit(nc, ops, waits, sem_of, sem_names, counters)

    def _emit(self, nc, ops, waits, sem_of, sem_names, finals):
        from contextlib import ExitStack
        with ExitStack() as ctx:
            sems = {}
            for name, _ in sem_names.values():
                sems[name] = ctx.enter_context(nc.semaphore(name))
            block = ctx.enter_context(nc.Block())
            engines = ["g", "y", "t", "v", "s"]
            secs = {"g": block.gpsimd, "y": block.sync, "t": block.tensor,
                    "v": block.vector, "s": block.scalar}

            def make_body(eng):
                def body(E):
                    waited = {}
                    for i, (e2, fn, rd, wr, dma, tag) in enumerate(ops):
                        if e2 != eng:
                            continue
                        for name, v in sorted(waits[i].items()):
                            if waited.get(name, -1) >= v:
                                continue
                            E.wait_ge(sems[name], v)
                            waited[name] = v
                        ins = fn(E)
                        if sem_of[i] is not None:
                            if ins is None:
                                raise RuntimeError(f"op {tag} needs inc but returned None")
                            name, amt = sem_of[i]
                            ins.then_inc(sems[name], amt)
                    for name in sorted(finals):
                        if waited.get(name, -1) < finals[name]:
                            E.wait_ge(sems[name], finals[name])
                return body
            for eng in engines:
                secs[eng](make_body(eng))


def build_nc(debug_taps=None, phases=5):
    nc = bass.Bass("TRN2", target_bir_lowering=False)

    d_x = nc.dram_tensor("x_own", [T, H], F32, kind="ExternalInput")
    d_xp = nc.dram_tensor("x_peer", [T, H], F32, kind="ExternalInput")
    d_wq = nc.dram_tensor("wq", [128, NH, NC, 384], F32, kind="ExternalInput")
    d_wga = nc.dram_tensor("wga", [128, 4, NC, 512], F32, kind="ExternalInput")
    d_wo = nc.dram_tensor("wo", [128, NC, H], F32, kind="ExternalInput")
    d_wgf = nc.dram_tensor("wgf", [128, 64, NC, 128], I8, kind="ExternalInput")
    d_wuf = nc.dram_tensor("wuf", [128, 64, NC, 128], I8, kind="ExternalInput")
    d_wdf = nc.dram_tensor("wdf", [128, 4, 16, 4, 512], I8, kind="ExternalInput")
    d_slope = nc.dram_tensor("slope", [1, NH], F32, kind="ExternalInput")
    d_normw = nc.dram_tensor("normw", [128, NC], F32, kind="ExternalInput")
    d_sg = nc.dram_tensor("sgate", [128, 64], F32, kind="ExternalInput")
    d_su = nc.dram_tensor("sup", [128, 64], F32, kind="ExternalInput")
    d_sd = nc.dram_tensor("sdown", [1, H], F32, kind="ExternalInput")
    d_mask = nc.dram_tensor("mask", [1, 1], F32, kind="ExternalInput")
    d_ident = nc.dram_tensor("identc", [128, 128], BF16, kind="ExternalInput")
    d_qcol = nc.dram_tensor("qcolc", [128, 2], F32, kind="ExternalInput")
    d_kcol = nc.dram_tensor("kcolc", [128, 2], F32, kind="ExternalInput")
    d_mnc = nc.dram_tensor("mncc", [128, 512], F32, kind="ExternalInput")
    d_out = nc.dram_tensor("out", [T, H], F32, kind="ExternalOutput")
    taps_out = {}
    if debug_taps:
        for name, shape, dt in debug_taps:
            taps_out[name] = nc.dram_tensor("tap_" + name, shape, dt,
                                            kind="ExternalOutput")

    ident = nc.alloc_sbuf_tensor("ident", [128, 128], BF16)
    slope_b = nc.alloc_sbuf_tensor("slope_b", [128, NH], F32)
    negs = nc.alloc_sbuf_tensor("negs", [128, NH], F32)
    qd = nc.alloc_sbuf_tensor("qd", [128, 2 * NH], F32)
    kd = nc.alloc_sbuf_tensor("kd", [128, 2 * NH], F32)
    bd = nc.alloc_sbuf_tensor("bd", [128, NH], F32)
    qcol = nc.alloc_sbuf_tensor("qcol", [128, 2], F32)
    kcol = nc.alloc_sbuf_tensor("kcol", [128, 2], F32)
    riota = nc.alloc_sbuf_tensor("riota", [128, 1], F32)
    mask_c = nc.alloc_sbuf_tensor("mask_c", [128, 1], F32)
    norm_c = nc.alloc_sbuf_tensor("norm_c", [128, NC], F32)
    sg_c = nc.alloc_sbuf_tensor("sg_c", [128, 64], F32)
    su_c = nc.alloc_sbuf_tensor("su_c", [128, 64], F32)
    sdb = nc.alloc_sbuf_tensor("sdb", [128, H], BF16)
    ss_all = nc.alloc_sbuf_tensor("ss_all", [128, 128], F32)
    rstd_all = nc.alloc_sbuf_tensor("rstd_all", [128, NT], F32)
    lnst = nc.alloc_sbuf_tensor("lnst", [128, 24], F32)
    gate_sb = nc.alloc_sbuf_tensor("gate_sb", [128, 1024], BF16)
    sdt_b = nc.alloc_sbuf_tensor("sdt_b", [128, 512], BF16)
    utmp = nc.alloc_sbuf_tensor("utmp", [128, 512], BF16)
    dtmp32 = nc.alloc_sbuf_tensor("dtmp32", [128, 2 * NH], F32)
    dgin = nc.alloc_sbuf_tensor("dgin", [128, 512], BF16)

    AR_BYTES = 200704
    arena = nc.alloc_sbuf_tensor("arena", [128, AR_BYTES], mybir.dt.uint8)
    base = nc.lookup_mloc(arena).addr

    def at(name, shape, dt, off):
        return nc.alloc_sbuf_tensor_at(name, shape, dt, offset=base + off)

    xnT = at("xnT", [128, NC * 1024], BF16, 0)
    A2 = 32768
    xnTp = at("xnTp", [128, NC * 1024], BF16, A2)
    gatedT = at("gatedT", [128, NC * 1024], BF16, A2)
    A3 = 65536
    o_all = at("o_all", [128, NT * H], F32, A3)
    x0 = at("x0", [128, H], F32, A3)
    x1 = at("x1", [128, H], F32, A3 + 8192)
    A4 = 131072
    wq_s = [at("wq_s0", [128, NC * 384], BF16, A4),
            at("wq_s1", [128, NC * 384], BF16, A4 + 12288)]
    dumpLN = at("dumpLN", [128, H], BF16, A3 + 16384)
    xn0 = at("xn0", [128, H], BF16, A3 + 20480)
    xn1 = at("xn1", [128, H], BF16, A3 + 24576)
    kpe = at("kpe", [128, T], BF16, A4 + 24576)
    vpe = at("vpe", [128, T], BF16, A4 + 26624)
    q_f2 = [at("q_f0", [128, T], BF16, A4 + 28672),
            at("q_f1", [128, T], BF16, A4 + 30720)]
    k_t2 = [at("k_t0", [128, T], BF16, A4 + 32768),
            at("k_t1", [128, T], BF16, A4 + 34816)]
    v_t2 = [at("v_t0", [128, T], BF16, A4 + 36864),
            at("v_t1", [128, T], BF16, A4 + 38912)]
    kdk_p = at("kdk_p", [128, 256], BF16, A4 + 40960)
    kdk_s = at("kdk_s", [128, 256], BF16, A4 + 41472)
    k_f = at("k_f", [128, 256], BF16, A4 + 41984)
    KV2 = [at("KV0", [128, 128], F32, A4 + 42496),
           at("KV1", [128, 128], F32, A4 + 43008)]
    S_b2 = [at("S_b0", [128, 128], BF16, A4 + 43520),
            at("S_b1", [128, 128], BF16, A4 + 43776)]
    mn = at("mn", [128, 512], F32, A4 + 46080)
    gated_tok2 = [at("gated_tok", [128, H], BF16, A4 + 65536),
                  at("gated_tokB", [128, H], BF16, A4 + 44032)]
    wg_s = at("wg_s", [128, 4 * NC * 512], BF16, A4)
    wo_q = [at("wo_q0", [128, 4 * H], BF16, A4),
            at("wo_q1", [128, 4 * H], BF16, A4 + 16384)]
    fuse = at("fuse", [128, 16 * 1024], BF16, A4)
    wd_q = [at("wd_q0", [128, 16 * 512], BF16, A2),
            at("wd_q1", [128, 16 * 512], BF16, A2 + 16384),
            at("wd_q2", [128, 16 * 512], BF16, A4 + 32768)]
    A5 = 180224
    xres = [at("xres0", [128, H], F32, A5),
            at("xres1", [128, H], F32, A5 + 8192)]
    wgf_s = [at("wgf_s0", [128, H], BF16, A5),
             at("wgf_s1", [128, H], BF16, A5 + 8192)]
    wuf_s = [at("wuf_s0", [128, H], BF16, A5 + 4096),
             at("wuf_s1", [128, H], BF16, A5 + 12288)]
    ost = [at("ost0", [128, 512], F32, A5 + 16384),
           at("ost1", [128, 512], F32, A5 + 18432)]
    diag_all = at("diag_all", [128, NH * 512], BF16, A5)

    pb = [nc.alloc_psum_tensor(f"pb{i}", [128, 512], F32) for i in range(7)]
    pbt = nc.alloc_psum_tensor("pbt", [128, 1024], BF16)

    B = Builder()

    B.add("g", lambda E: E.dma_start(slope_b[:, :], d_slope.ap().broadcast_to((128, NH))),
          writes=["slope_b"], dma=True, tag="ld_slope")
    B.add("g", lambda E: E.dma_start(mask_c[:, :], d_mask.ap().broadcast_to((128, 1))),
          writes=["mask_c"], dma=True, tag="ld_mask")
    B.add("g", lambda E: E.dma_start(sdb[:, :], d_sd.ap().broadcast_to((128, H))),
          writes=["sdb"], dma=True, tag="ld_sdb")
    B.add("y", lambda E: E.dma_start(norm_c[:, :], d_normw[:, :]),
          writes=["norm_c"], dma=True, tag="ld_norm")
    B.add("y", lambda E: E.dma_start(sg_c[:, :], d_sg[:, :]),
          writes=["sg_c"], dma=True, tag="ld_sg")
    B.add("y", lambda E: E.dma_start(su_c[:, :], d_su[:, :]),
          writes=["su_c"], dma=True, tag="ld_su")

    B.add("y", lambda E: E.dma_start(qcol[:, :], d_qcol[:, :]),
          writes=["qcol"], dma=True, tag="ld_qcol")
    B.add("y", lambda E: E.dma_start(kcol[:, :], d_kcol[:, :]),
          writes=["kcol"], dma=True, tag="ld_kcol")
    B.add("y", lambda E: E.dma_start(ident[:, :], d_ident[:, :]),
          writes=["ident"], dma=True, tag="ld_ident")
    B.add("y", lambda E: E.dma_start(mn[:, :], d_mnc[:, :]),
          writes=["mn"], dma=True, tag="ld_mn")
    B.add("v", lambda E: E.tensor_scalar_mul(negs[:, :], slope_b[:, :], -1.0),
          reads=["slope_b"], writes=["negs"], tag="negs")
    for par in range(2):
        B.add("v", lambda E, par=par: E.tensor_scalar_mul(
            dtmp32[:, par * NH:(par + 1) * NH], slope_b[:, :],
            qcol[:, par:par + 1]),
            reads=["slope_b", "qcol"], writes=["dtq"], tag=f"dtq{par}")
    for par in range(2):
        B.add("s", lambda E, par=par: E.activation(
            qd[:, par * NH:(par + 1) * NH],
            dtmp32[:, par * NH:(par + 1) * NH], AF.Exp),
            reads=["dtq"], writes=["qd"], tag=f"qd{par}")
    for par in range(2):
        B.add("v", lambda E, par=par: E.tensor_scalar_mul(
            dtmp32[:, par * NH:(par + 1) * NH], slope_b[:, :],
            kcol[:, par:par + 1]),
            reads=["slope_b", "kcol", "qd"], writes=["dtk"], tag=f"dtk{par}")
    for par in range(2):
        B.add("s", lambda E, par=par: E.activation(
            kd[:, par * NH:(par + 1) * NH],
            dtmp32[:, par * NH:(par + 1) * NH], AF.Exp),
            reads=["dtk"], writes=["kd"], tag=f"kd{par}")
    B.add("s", lambda E: E.activation(bd[:, :], slope_b[:, :], AF.Exp,
                                      scale=-float(BLK)),
          reads=["slope_b"], writes=["bd"], tag="bd")

    def wq_load(h):
        slot = h % 2
        extra = []
        B.add("g", lambda E, h=h, slot=slot: E.dma_start(
            wq_s[slot][:, :].rearrange("p (c o) -> p c o", o=384),
            d_wq[:, h, :, :]),
            writes=[f"wq{slot}"] + extra, dma=True, tag=f"wq_ld{h}")

    def ln_pass(src_dram, dst_T, dst_key, pref, load_eng="g", tr_eng="y"):
        for tt in range(NT):
            s = tt % 2
            xsl = x0 if s == 0 else x1
            xk = f"x{s}"
            lk = lambda col: f"ln_{s}_{col}"
            B.add(load_eng, lambda E, tt=tt, xsl=xsl, src=src_dram:
                  E.dma_start(xsl[:, :], src[tt * 128:(tt + 1) * 128, :]),
                  writes=[xk], dma=True, tag=f"{pref}_x{tt}")
            B.add("s", lambda E, xsl=xsl, s=s:
                  E.activation(dumpLN[:, :], xsl[:, :], AF.Identity,
                               accum_out=lnst[:, 12 * s:12 * s + 1]),
                  reads=[xk], writes=[lk("sum"), "dumpLN"], tag=f"{pref}_sum{tt}")
            B.add("s", lambda E, xsl=xsl, s=s:
                  E.activation(dumpLN[:, :], xsl[:, :], AF.Square,
                               accum_out=lnst[:, 12 * s + 1:12 * s + 2]),
                  reads=[xk], writes=[lk("sq"), "dumpLN"], tag=f"{pref}_sq{tt}")
            B.add("v", lambda E, s=s:
                  E.tensor_scalar_mul(lnst[:, 12 * s + 2:12 * s + 3],
                                      lnst[:, 12 * s:12 * s + 1], 1.0 / H),
                  reads=[lk("sum")], writes=[lk("mean")], tag=f"{pref}_mean{tt}")
            B.add("v", lambda E, s=s:
                  E.tensor_scalar_mul(lnst[:, 12 * s + 3:12 * s + 4],
                                      lnst[:, 12 * s + 1:12 * s + 2], 1.0 / H),
                  reads=[lk("sq")], writes=[lk("e2")], tag=f"{pref}_e2{tt}")
            B.add("v", lambda E, s=s:
                  E.tensor_mul(lnst[:, 12 * s + 4:12 * s + 5],
                               lnst[:, 12 * s + 2:12 * s + 3],
                               lnst[:, 12 * s + 2:12 * s + 3]),
                  reads=[lk("mean")], writes=[lk("msq")], tag=f"{pref}_msq{tt}")
            B.add("v", lambda E, s=s:
                  E.tensor_sub(lnst[:, 12 * s + 5:12 * s + 6],
                               lnst[:, 12 * s + 3:12 * s + 4],
                               lnst[:, 12 * s + 4:12 * s + 5]),
                  reads=[lk("e2"), lk("msq")], writes=[lk("veps")],
                  tag=f"{pref}_var{tt}")
            B.add("v", lambda E, s=s:
                  E.tensor_scalar_add(lnst[:, 12 * s + 5:12 * s + 6],
                                      lnst[:, 12 * s + 5:12 * s + 6], 1e-5),
                  reads=[lk("veps")], writes=[lk("veps")], tag=f"{pref}_veps{tt}")
            B.add("s", lambda E, s=s:
                  E.activation(lnst[:, 12 * s + 6:12 * s + 7],
                               lnst[:, 12 * s + 5:12 * s + 6], AF.Ln),
                  reads=[lk("veps")], writes=[lk("lnv")], tag=f"{pref}_ln{tt}")
            B.add("s", lambda E, s=s:
                  E.activation(lnst[:, 12 * s + 7:12 * s + 8],
                               lnst[:, 12 * s + 6:12 * s + 7], AF.Exp,
                               scale=-0.5),
                  reads=[lk("lnv")], writes=[lk("rstd")], tag=f"{pref}_rstd{tt}")
            B.add("v", lambda E, s=s:
                  E.tensor_mul(lnst[:, 12 * s + 8:12 * s + 9],
                               lnst[:, 12 * s + 2:12 * s + 3],
                               lnst[:, 12 * s + 7:12 * s + 8]),
                  reads=[lk("mean"), lk("rstd")], writes=[lk("nb")],
                  tag=f"{pref}_nb0{tt}")
            B.add("v", lambda E, s=s:
                  E.tensor_scalar_mul(lnst[:, 12 * s + 8:12 * s + 9],
                                      lnst[:, 12 * s + 8:12 * s + 9], -1.0),
                  reads=[lk("nb")], writes=[lk("nb")], tag=f"{pref}_nb{tt}")
            xnsl = xn0 if s == 0 else xn1
            xnk = f"xn{s}"
            B.add("v", lambda E, xsl=xsl, xnsl=xnsl, s=s:
                  E.tensor_scalar(xnsl[:, :], xsl[:, :],
                                  lnst[:, 12 * s + 7:12 * s + 8],
                                  lnst[:, 12 * s + 8:12 * s + 9],
                                  ALU.mult, ALU.add),
                  reads=[xk, lk("rstd"), lk("nb")], writes=[xnk],
                  tag=f"{pref}_xn{tt}")
            B.add(tr_eng, lambda E, tt=tt, xnsl=xnsl, dst_T=dst_T:
                  E.dma_start(
                      dst_T[:, :].rearrange("p (c t) -> p c t", t=1024)[:, :, tt * 128:(tt + 1) * 128],
                      xnsl[:, :].rearrange("p (c f) -> p c f", f=128),
                      transpose=True),
                  reads=[xnk], writes=[f"{dst_key}_{tt}"], dma=True,
                  tag=f"{pref}_tr{tt}")

    if phases >= 1:
        ln_pass(d_x, xnT, "xnT", "lno")
        if phases >= 2:
            wq_load(0)
            wq_load(1)
            for h in range(NH):
                B.add("v", lambda E, h=h: E.tensor_scalar_mul(
                    dgin[:, :], mn[:, :], negs[:, h:h + 1]),
                    reads=["mn", "negs", "dgin_rd"], writes=["dgin"],
                    tag=f"dgin{h}")
                B.add("s", lambda E, h=h: E.activation(
                    diag_all[:, h * 512:(h + 1) * 512], dgin[:, :], AF.Exp),
                    reads=["dgin"], writes=["diag_all", "dgin_rd"],
                    tag=f"dexp{h}")
        ln_pass(d_xp, xnTp, "xnTp", "lnp")

    def fill_chunks(h):
        p = h % 2
        wqk = f"wq{p}"
        W = wq_s[p]
        qk_, kk, vk = f"q_f{p}", f"k_t{p}", f"v_t{p}"
        chunks = []

        def q_chunk(td, h=h, p=p, W=W, wqk=wqk, qk_=qk_):
            pk = f"pb{td}"
            for c in range(NC):
                B.add("t", lambda E, c=c, td=td, W=W: E.matmul(
                    pb[td][:, :], W[:, c * 384:c * 384 + 128],
                    xnT[:, c * 1024 + td * 512:c * 1024 + (td + 1) * 512],
                    start=(c == 0), stop=(c == NC - 1)),
                    reads=[wqk] + [f"xnT_{t}" for t in range(4 * td, 4 * td + 4)],
                    writes=[pk], tag=f"qmm{h}_{td}_{c}")
            B.add("s", lambda E, td=td, p=p: E.activation(
                q_f2[p][:, td * 512:(td + 1) * 512], pb[td][:, :], AF.Silu),
                reads=[pk], writes=[qk_], tag=f"qsilu{h}_{td}")
        chunks.append(lambda: q_chunk(0))
        chunks.append(lambda: q_chunk(1))

        def kv_chunk(xsrc, xkey, kdst, vdst, kk2, vk2, tts, pref, h=h, W=W,
                     wqk=wqk):
            for tt in tts:
                pk = f"pb{tt % 2}"
                for c in range(NC):
                    B.add("t", lambda E, c=c, tt=tt, xsrc=xsrc, W=W: E.matmul(
                        pb[tt % 2][:, 0:256],
                        xsrc[:, c * 1024 + tt * 128:c * 1024 + (tt + 1) * 128],
                        W[:, c * 384 + 128:c * 384 + 384],
                        start=(c == 0), stop=(c == NC - 1)),
                        reads=[wqk, f"{xkey}_{tt}"], writes=[pk],
                        tag=f"kvmm{pref}{h}_{tt}_{c}")
                B.add("s", lambda E, tt=tt, kdst=kdst: E.activation(
                    kdst[:, tt * 128:(tt + 1) * 128],
                    pb[tt % 2][:, 0:128], AF.Silu),
                    reads=[pk], writes=[kk2], tag=f"ksilu{pref}{h}_{tt}")
                B.add("s", lambda E, tt=tt, vdst=vdst: E.activation(
                    vdst[:, tt * 128:(tt + 1) * 128],
                    pb[tt % 2][:, 128:256], AF.Silu),
                    reads=[pk], writes=[vk2], tag=f"vsilu{pref}{h}_{tt}")
        for tts in ((0, 1), (2, 3), (4, 5), (6, 7)):
            chunks.append(lambda tts=tts, p=p: kv_chunk(
                xnT, "xnT", k_t2[p], v_t2[p], f"k_t{p}", f"v_t{p}", tts, "own"))
        for tts in ((0, 1), (2, 3), (4, 5), (6, 7)):
            chunks.append(lambda tts=tts: kv_chunk(
                xnTp, "xnTp", kpe, vpe, "kpe", "vpe", tts, "pe"))

        def prefix_chunk(j, h=h, p=p):
            kvk = f"KV{p}"
            for nck in range(2):
                B.add("v", lambda E, j=j, nck=nck, h=h: E.tensor_scalar_mul(
                    kdk_p[:, nck * 128:(nck + 1) * 128],
                    kpe[:, (2 * j + nck) * 128:(2 * j + nck + 1) * 128],
                    kd[:, nck * NH + h:nck * NH + h + 1]),
                    reads=["kpe", "kd"], writes=["kdk_p"],
                    tag=f"pkdk{h}_{j}_{nck}")
            for nck in range(2):
                B.add("t", lambda E, j=j, nck=nck: E.matmul(
                    pb[3][:, 0:128], kdk_p[:, nck * 128:(nck + 1) * 128],
                    vpe[:, (2 * j + nck) * 128:(2 * j + nck + 1) * 128],
                    start=(nck == 0), stop=(nck == 1)),
                    reads=["kdk_p", "vpe"], writes=["pb3"],
                    tag=f"pC{h}_{j}_{nck}")
            if j == 0:
                B.add("v", lambda E, p=p: E.tensor_copy(KV2[p][:, :],
                                                        pb[3][:, 0:128]),
                      reads=["pb3"], writes=[kvk], tag=f"pKV{h}_0")
            else:
                B.add("v", lambda E, h=h, p=p: E.scalar_tensor_tensor(
                    KV2[p][:, :], KV2[p][:, :], bd[:, h:h + 1],
                    pb[3][:, 0:128], ALU.mult, ALU.add),
                    reads=["pb3", kvk, "bd"], writes=[kvk],
                    tag=f"pKV{h}_{j}")
            if j == NB - 1:
                B.add("v", lambda E, p=p: E.tensor_scalar_mul(
                    KV2[p][:, :], KV2[p][:, :], mask_c[:, 0:1]),
                    reads=[kvk, "mask_c"], writes=[kvk], tag=f"maskKV{h}")
                B.add("v", lambda E, p=p: E.tensor_copy(S_b2[p][:, :],
                                                        KV2[p][:, :]),
                      reads=[kvk], writes=[f"S_b{p}"], tag=f"Sb{h}_init")
        for j in range(NB):
            chunks.append(lambda j=j: prefix_chunk(j))
        return chunks

    def scan_block_segs(h, j):
        p = h % 2
        qk_, kk, vk = f"q_f{p}", f"k_t{p}", f"v_t{p}"
        kvk, sbk, dk = f"KV{p}", f"S_b{p}", "diag_all"

        def seg1(h=h, j=j, p=p):
            for nck in range(2):
                B.add("t", lambda E, j=j, nck=nck, p=p: E.transpose(
                    pbt[:, nck * 128:(nck + 1) * 128],
                    k_t2[p][:, (2 * j + nck) * 128:(2 * j + nck + 1) * 128],
                    ident[:, :]),
                    reads=[kk, "ident"], writes=["pbt"],
                    tag=f"ktr{h}_{j}_{nck}")
            B.add("v", lambda E: E.tensor_copy(k_f[:, :], pbt[:, 0:256]),
                  reads=["pbt"], writes=["k_f"], tag=f"kf{h}_{j}")

        def seg2(h=h, j=j, p=p):
            for nck in range(2):
                B.add("t", lambda E, j=j, nck=nck, p=p: E.matmul(
                    pb[2][:, nck * 256:(nck + 1) * 256],
                    k_f[:, nck * 128:(nck + 1) * 128],
                    q_f2[p][:, j * 256:(j + 1) * 256],
                    start=True, stop=True),
                    reads=["k_f", qk_], writes=["pb2"],
                    tag=f"st{h}_{j}_{nck}")
            B.add("v", lambda E, h=h: E.tensor_mul(
                sdt_b[:, :], pb[2][:, :], diag_all[:, h * 512:(h + 1) * 512]),
                reads=["pb2", dk], writes=["sdt"], tag=f"sdt{h}_{j}")

        def seg3(h=h, j=j, p=p):
            for mc in range(2):
                bank = 4 + mc
                for nck in range(2):
                    B.add("t", lambda E, j=j, mc=mc, nck=nck, bank=bank, p=p:
                          E.matmul(
                              pb[bank][:, 0:128],
                              sdt_b[:, nck * 256 + mc * 128:nck * 256 + (mc + 1) * 128],
                              v_t2[p][:, (2 * j + nck) * 128:(2 * j + nck + 1) * 128],
                              start=(nck == 0), stop=(nck == 1)),
                          reads=["sdt", vk], writes=[f"pb{bank}"],
                          tag=f"op{h}_{j}_{mc}_{nck}")
            for mc in range(2):
                B.add("t", lambda E, j=j, mc=mc, p=p: E.matmul(
                    pb[6][:, mc * 128:(mc + 1) * 128],
                    q_f2[p][:, j * 256 + mc * 128:j * 256 + (mc + 1) * 128],
                    S_b2[p][:, :], start=True, stop=True),
                    reads=[qk_, sbk], writes=["pb6"],
                    tag=f"oi{h}_{j}_{mc}")
            for mc in range(2):
                tt = 2 * j + mc
                col = tt * H + h * 128
                B.add("v", lambda E, col=col, mc=mc, h=h: E.tensor_scalar_mul(
                    o_all[:, col:col + 128], pb[6][:, mc * 128:(mc + 1) * 128],
                    qd[:, mc * NH + h:mc * NH + h + 1]),
                    reads=["pb6", "qd"],
                    writes=(["o_all", "x0", "x1", "dumpLN", "xn0", "xn1"]
                            if (h == 0 and j == 0 and mc == 0) else ["o_all"]),
                    tag=f"evI{h}_{j}_{mc}")
                B.add("v", lambda E, col=col, mc=mc: E.tensor_add(
                    o_all[:, col:col + 128], o_all[:, col:col + 128],
                    pb[4 + mc][:, 0:128]),
                    reads=[f"pb{4 + mc}", "o_all"], writes=["o_all"],
                    tag=f"evP{h}_{j}_{mc}")
                B.add("s", lambda E, col=col, tt=tt, h=h: E.activation(
                    gate_sb[:, 512:640], o_all[:, col:col + 128], AF.Square,
                    accum_out=ss_all[:, tt * NH + h:tt * NH + h + 1]),
                    reads=["o_all"], writes=["ssdump", "ss_all"],
                    tag=f"ssq{h}_{j}_{mc}")

        def seg4(h=h, j=j, p=p):
            if j == NB - 1:
                return
            for nck in range(2):
                B.add("v", lambda E, j=j, nck=nck, h=h, p=p: E.tensor_scalar_mul(
                    kdk_s[:, nck * 128:(nck + 1) * 128],
                    k_t2[p][:, (2 * j + nck) * 128:(2 * j + nck + 1) * 128],
                    kd[:, nck * NH + h:nck * NH + h + 1]),
                    reads=[kk, "kd"], writes=["kdk_s"],
                    tag=f"kdk{h}_{j}_{nck}")
            for nck in range(2):
                B.add("t", lambda E, j=j, nck=nck, p=p: E.matmul(
                    pb[3][:, 0:128], kdk_s[:, nck * 128:(nck + 1) * 128],
                    v_t2[p][:, (2 * j + nck) * 128:(2 * j + nck + 1) * 128],
                    start=(nck == 0), stop=(nck == 1)),
                    reads=["kdk_s", vk], writes=["pb3"],
                    tag=f"C{h}_{j}_{nck}")
            B.add("v", lambda E, h=h, p=p: E.scalar_tensor_tensor(
                KV2[p][:, :], KV2[p][:, :], bd[:, h:h + 1], pb[3][:, 0:128],
                ALU.mult, ALU.add),
                reads=["pb3", kvk, "bd"], writes=[kvk],
                tag=f"KV{h}_{j}")
            B.add("v", lambda E, p=p: E.tensor_copy(S_b2[p][:, :],
                                                    KV2[p][:, :]),
                  reads=[kvk], writes=[sbk], tag=f"Sb{h}_{j}")
        return [seg1, seg2, seg3, seg4]

    if phases >= 2:
        for c in fill_chunks(0):
            c()
        for c in fill_chunks(1):
            c()
        for h in range(NH):
            if h + 2 < NH:
                wq_load(h + 2)
            pending = fill_chunks(h + 2) if h + 2 < NH else []
            segs = [s for j in range(NB) for s in scan_block_segs(h, j)]
            nslots = len(segs)
            k = 0
            for si, seg in enumerate(segs):
                seg()
                want = ((si + 1) * len(pending)) // nslots
                while k < want:
                    pending[k]()
                    k += 1
            while k < len(pending):
                pending[k]()
                k += 1

    if phases >= 3:
        for tt in range(NT):
            B.add("s", lambda E, tt=tt: E.activation(
                gate_sb[:, 0:NH], ss_all[:, tt * NH:(tt + 1) * NH], AF.Identity,
                accum_out=lnst[:, 0:1]),
                reads=["ss_all", "gdump"], writes=["r3sum", "gdump"], tag=f"r3s{tt}")
            B.add("v", lambda E: E.tensor_scalar(lnst[:, 2:3], lnst[:, 0:1],
                                                 1.0 / H, 1e-6, ALU.mult, ALU.add),
                  reads=["r3sum"], writes=["r3me"], tag=f"r3m{tt}")
            B.add("s", lambda E: E.activation(lnst[:, 1:2], lnst[:, 2:3], AF.Ln),
                  reads=["r3me"], writes=["r3ln"], tag=f"r3l{tt}")
            B.add("s", lambda E, tt=tt: E.activation(
                rstd_all[:, tt:tt + 1], lnst[:, 1:2], AF.Exp, scale=-0.5),
                reads=["r3ln"], writes=["rstd_all"], tag=f"r3e{tt}")
        B.add("g", lambda E: E.dma_start(
            wg_s[:, 0:14336].rearrange("p (a o) -> p a o", o=512),
            d_wga.ap().rearrange("p a c o -> p (a c) o")[:, 0:28, :]),
            writes=["wg_A", "wq0", "wq1", "kpe", "vpe"],
            dma=True, tag="wg_ldA")
        B.add("g", lambda E: E.dma_start(
            wg_s[:, 14336:].rearrange("p (a o) -> p a o", o=512),
            d_wga.ap().rearrange("p a c o -> p (a c) o")[:, 28:, :]),
            writes=["wg_B", "q_f0", "q_f1", "diag_all"],
            dma=True, tag="wg_ldB")
        for tt in range(NT):
            for ot in range(4):
                pk = f"pb{ot % 2}"
                for c in range(NC):
                    B.add("t", lambda E, tt=tt, ot=ot, c=c: E.matmul(
                        pb[ot % 2][:, :],
                        xnT[:, c * 1024 + tt * 128:c * 1024 + (tt + 1) * 128],
                        wg_s[:, (ot * NC + c) * 512:(ot * NC + c + 1) * 512],
                        start=(c == 0), stop=(c == NC - 1)),
                        reads=[f"xnT_{tt}",
                               "wg_A" if (ot * NC + c) < 28 else "wg_B"],
                        writes=[pk], tag=f"gmm{tt}_{ot}_{c}")
                gsl = ot % 2
                B.add("s", lambda E, ot=ot, gsl=gsl: E.activation(
                    gate_sb[:, gsl * 512:(gsl + 1) * 512], pb[ot % 2][:, :],
                    AF.Sigmoid),
                    reads=[pk], writes=[f"gate{gsl}"], tag=f"gsig{tt}_{ot}")
                tsl = tt % 2
                gtk = f"gated_tok{tsl}"
                extra = (["k_t0", "k_t1"] if (tt == 0 and ot == 0) else
                         (["mn"] if (tt == 1 and ot == 0) else []))
                B.add("v", lambda E, tt=tt, ot=ot, gsl=gsl, tsl=tsl:
                      E.scalar_tensor_tensor(
                          gated_tok2[tsl][:, ot * 512:(ot + 1) * 512],
                          o_all[:, tt * H + ot * 512:tt * H + (ot + 1) * 512],
                          rstd_all[:, tt:tt + 1],
                          gate_sb[:, gsl * 512:(gsl + 1) * 512],
                          ALU.mult, ALU.mult),
                      reads=["o_all", "rstd_all", f"gate{gsl}"],
                      writes=[gtk] + extra, tag=f"gated{tt}_{ot}")
            extra2 = ["xnTp"] if tt == 0 else []
            B.add("y", lambda E, tt=tt, tsl=tt % 2: E.dma_start(
                gatedT[:, :].rearrange("p (c t) -> p c t", t=1024)[:, :, tt * 128:(tt + 1) * 128],
                gated_tok2[tsl][:, :].rearrange("p (c f) -> p c f", f=128),
                transpose=True),
                reads=[f"gated_tok{tt % 2}"], writes=["gatedT"] + extra2,
                dma=True, tag=f"gtr{tt}")

    if phases >= 4:
        def wo_fetch(qo):
            slot = qo % 2
            extra = (["wg_A", "wg_B"] if qo in (0, 1) else [])
            wok = f"wo_q{slot}"
            B.add("g", lambda E, qo=qo, slot=slot: E.dma_start(
                wo_q[slot][:, :].rearrange("p (a f) -> p a f", f=H),
                d_wo[:, qo * 4:(qo + 1) * 4, :]),
                writes=[wok] + extra, dma=True, tag=f"wo_ld{qo}")
            for l in range(4):
                B.add("v", lambda E, qo=qo, slot=slot, l=l: E.tensor_scalar_mul(
                    wo_q[slot][:, l * H:(l + 1) * H],
                    wo_q[slot][:, l * H:(l + 1) * H],
                    norm_c[:, qo * 4 + l:qo * 4 + l + 1]),
                    reads=[wok, "norm_c"], writes=[wok], tag=f"wosc{qo}_{l}")

        wo_fetch(0)
        for qo in range(4):
            slot = qo % 2
            wok = f"wo_q{slot}"
            for tt in range(NT):
                if tt == 4 and qo + 1 < 4:
                    wo_fetch(qo + 1)
                if qo == 0:
                    exx = ["wg_A", "wg_B"] if tt in (0, 1) else []
                    B.add("y", lambda E, tt=tt: E.dma_start(
                        xres[tt % 2][:, :], d_x[tt * 128:(tt + 1) * 128, :]),
                        writes=[f"xres{tt % 2}"] + exx, dma=True, tag=f"xres{tt}")
                for ft in range(4):
                    pk = f"pb{ft % 2}"
                    for l in range(4):
                        B.add("t", lambda E, qo=qo, slot=slot, tt=tt, ft=ft, l=l:
                              E.matmul(
                                  pb[ft % 2][:, :],
                                  gatedT[:, (qo * 4 + l) * 1024 + tt * 128:(qo * 4 + l) * 1024 + (tt + 1) * 128],
                                  wo_q[slot][:, l * H + ft * 512:l * H + (ft + 1) * 512],
                                  start=(l == 0), stop=(l == 3)),
                              reads=["gatedT", wok], writes=[pk],
                              tag=f"omm{qo}_{tt}_{ft}_{l}")
                    col = tt * H + ft * 512
                    if qo == 0:
                        B.add("v", lambda E, col=col, ft=ft, tt=tt: E.tensor_add(
                            o_all[:, col:col + 512], xres[tt % 2][:, ft * 512:(ft + 1) * 512],
                            pb[ft % 2][:, :]),
                            reads=[pk, f"xres{tt % 2}"], writes=["o_all"],
                            tag=f"acc0_{tt}_{ft}")
                    else:
                        B.add("v", lambda E, col=col, ft=ft: E.tensor_add(
                            o_all[:, col:col + 512], o_all[:, col:col + 512],
                            pb[ft % 2][:, :]),
                            reads=[pk, "o_all"], writes=["o_all"],
                            tag=f"acc{qo}_{tt}_{ft}")

    if phases >= 5:
        st_ct = [0]

        def wd_fetch(u):
            qq, hh = divmod(u, 4)
            wsl = u % 3
            wdk = f"wd_q{wsl}"
            extra = (["gatedT"] if u in (0, 1) else
                     (["k_t0", "k_t1", "v_t0", "v_t1", "kdk_p", "kdk_s",
                       "k_f", "KV0", "KV1", "S_b0", "S_b1", "gated_tok1",
                       "mn"] if u == 2 else []))
            B.add("g", lambda E, qq=qq, hh=hh, wsl=wsl: E.dma_start(
                wd_q[wsl][:, :].rearrange("p (c i) -> p c i", i=512),
                d_wdf[:, qq, :, hh, :]),
                writes=[wdk] + [f"{wdk}_{ic}" for ic in range(16)] + extra,
                dma=True, tag=f"wd{qq}_{hh}")
            for ic in range(16):
                B.add("v", lambda E, ic=ic, hh=hh, wsl=wsl: E.tensor_mul(
                    wd_q[wsl][:, ic * 512:(ic + 1) * 512],
                    wd_q[wsl][:, ic * 512:(ic + 1) * 512],
                    sdb[:, hh * 512:(hh + 1) * 512]),
                    reads=[wdk, "sdb"], writes=[f"{wdk}_{ic}"],
                    tag=f"wdsc{qq}_{hh}_{ic}")

        for q in range(4):
            for ic in range(16):
                gi = q * 16 + ic
                sl = gi % 2
                extra = ([f"xres{sl}", "wg_A", "wg_B"] if gi in (0, 1) else [])
                B.add("g", lambda E, gi=gi, sl=sl: E.dma_start(
                    wgf_s[sl][:, :].rearrange("p (c i) -> p c i", i=128),
                    d_wgf[:, gi, :, :]),
                    writes=[f"wgf{sl}"] + extra, dma=True, tag=f"wgf{gi}")
                B.add("g", lambda E, gi=gi, sl=sl: E.dma_start(
                    wuf_s[sl][:, :].rearrange("p (c i) -> p c i", i=128),
                    d_wuf[:, gi, :, :]),
                    writes=[f"wuf{sl}"] + extra, dma=True, tag=f"wuf{gi}")
                for td in range(2):
                    gk, uk = f"pb{td}", f"pb{4 + td}"
                    for c in range(NC):
                        B.add("t", lambda E, sl=sl, c=c, td=td: E.matmul(
                            pb[td][:, :], wgf_s[sl][:, c * 128:(c + 1) * 128],
                            xnT[:, c * 1024 + td * 512:c * 1024 + (td + 1) * 512],
                            start=(c == 0), stop=(c == NC - 1)),
                            reads=[f"wgf{sl}"] + [f"xnT_{t}" for t in range(4 * td, 4 * td + 4)],
                            writes=[gk], tag=f"fg{gi}_{td}_{c}")
                    for c in range(NC):
                        B.add("t", lambda E, sl=sl, c=c, td=td: E.matmul(
                            pb[4 + td][:, :], wuf_s[sl][:, c * 128:(c + 1) * 128],
                            xnT[:, c * 1024 + td * 512:c * 1024 + (td + 1) * 512],
                            start=(c == 0), stop=(c == NC - 1)),
                            reads=[f"wuf{sl}"] + [f"xnT_{t}" for t in range(4 * td, 4 * td + 4)],
                            writes=[uk], tag=f"fu{gi}_{td}_{c}")
                    fcol = ic * 1024 + td * 512
                    extra2 = (["wo_q0", "wo_q1"] if gi == 0 and td == 0 else [])
                    B.add("s", lambda E, fcol=fcol, gi=gi, td=td: E.activation(
                        fuse[:, fcol:fcol + 512], pb[td][:, :], AF.Silu,
                        scale=sg_c[:, gi:gi + 1]),
                        reads=[gk, "sg_c"], writes=["fuse"] + extra2,
                        tag=f"fsilu{gi}_{td}")
                    B.add("s", lambda E, gi=gi, td=td: E.activation(
                        utmp[:, :], pb[4 + td][:, :],
                        AF.Identity, scale=su_c[:, gi:gi + 1]),
                        reads=[uk, "su_c"], writes=["utmp"], tag=f"fup{gi}_{td}")
                    B.add("v", lambda E, fcol=fcol, td=td: E.tensor_mul(
                        fuse[:, fcol:fcol + 512], fuse[:, fcol:fcol + 512],
                        utmp[:, :]),
                        reads=["fuse", "utmp"], writes=["fuse"],
                        tag=f"fmul{gi}_{td}")
            for ht in range(4):
                u = q * 4 + ht
                wsl = u % 3
                wdk = f"wd_q{wsl}"
                if ht == 0 and q == 0:
                    wd_fetch(0)
                    wd_fetch(1)
                for tt in range(NT):
                    if tt == 2 and u + 2 < 16:
                        wd_fetch(u + 2)
                    pk = f"pb{2 + tt % 2}"
                    for ic in range(16):
                        B.add("t", lambda E, ic=ic, tt=tt, ht=ht, wsl=wsl: E.matmul(
                            pb[2 + tt % 2][:, :],
                            fuse[:, ic * 1024 + tt * 128:ic * 1024 + (tt + 1) * 128],
                            wd_q[wsl][:, ic * 512:(ic + 1) * 512],
                            start=(ic == 0), stop=(ic == 15)),
                            reads=["fuse", f"{wdk}_{ic}"], writes=[pk],
                            tag=f"dmm{q}_{ht}_{tt}_{ic}")
                    col = tt * H + ht * 512
                    if q < 3:
                        B.add("v", lambda E, col=col, tt=tt: E.tensor_add(
                            o_all[:, col:col + 512], o_all[:, col:col + 512],
                            pb[2 + tt % 2][:, :]),
                            reads=[pk, "o_all"], writes=["o_all"],
                            tag=f"dacc{q}_{ht}_{tt}")
                    else:
                        osl = st_ct[0] % 2
                        exo = ["gated_tok0"] if st_ct[0] in (0, 1) else []
                        B.add("v", lambda E, col=col, tt=tt, osl=osl: E.tensor_add(
                            ost[osl][:, :], o_all[:, col:col + 512],
                            pb[2 + tt % 2][:, :]),
                            reads=[pk, "o_all"], writes=[f"ost{osl}"] + exo,
                            tag=f"fin{ht}_{tt}")
                        B.add("y", lambda E, tt=tt, ht=ht, osl=osl: E.dma_start(
                            d_out[tt * 128:(tt + 1) * 128, ht * 512:(ht + 1) * 512],
                            ost[osl][:, :]),
                            reads=[f"ost{osl}"], writes=["OUT"], dma=True,
                            tag=f"st{ht}_{tt}")
                        st_ct[0] += 1

    if debug_taps:
        ctx = dict(xnT=xnT, xnTp=xnTp, o_all=o_all, q_f=q_f, k_tok=k_tok, lnst=lnst,
                   v_tok=v_tok, kpe=kpe, vpe=vpe, KV=KV, S_b=S_b, qd=qd,
                   kd=kd, bd=bd, diag=diag, ss_all=ss_all, rstd_all=rstd_all,
                   gatedT=gatedT, fuse=fuse, sdb=sdb, gate_sb=gate_sb)
        for name, shape, dt in debug_taps:
            src, rkeys = TAP_SPECS[name](ctx)
            B.add("y", lambda E, src=src, name=name: E.dma_start(
                taps_out[name].ap(), src), reads=rkeys, writes=["TAPS"],
                dma=True, tag=f"tap_{name}")

    B.add("v", lambda E: E.engine_nop(), reads=["OUT", "TAPS"] if debug_taps
          else ["OUT"], tag="final_wait")

    B.finalize(nc)
    return nc


TAP_SPECS = {}


def _layout_inputs(L0_attn_hidden_states, L0_attn_slope_rate, L0_attn_w_qkv,
                   L0_attn_w_gate, L0_attn_w_out, L0_attn_norm_w,
                   L0_ffn_w_gate, L0_ffn_w_up, L0_ffn_w_down,
                   L0_ffn_s_gate, L0_ffn_s_up, L0_ffn_s_down):
    c = np.ascontiguousarray
    x = np.asarray(L0_attn_hidden_states, dtype=np.float32)
    wq = c(np.asarray(L0_attn_w_qkv, np.float32).reshape(16, 384, 16, 128)
           .transpose(3, 0, 2, 1))
    wga = c(np.asarray(L0_attn_w_gate, np.float32).reshape(4, 512, 16, 128)
            .transpose(3, 0, 2, 1))
    wo = c(np.asarray(L0_attn_w_out, np.float32).T.reshape(16, 128, 2048)
           .transpose(1, 0, 2))
    wgf = c(np.asarray(L0_ffn_w_gate, np.int8).reshape(16, 128, 64, 128)
            .transpose(1, 2, 0, 3))
    wuf = c(np.asarray(L0_ffn_w_up, np.int8).reshape(16, 128, 64, 128)
            .transpose(1, 2, 0, 3))
    wdf = c(np.asarray(L0_ffn_w_down, np.int8).reshape(4, 16, 128, 4, 512)
            .transpose(2, 0, 1, 3, 4))
    slope = c(np.asarray(L0_attn_slope_rate, np.float32).reshape(1, 16))
    normw = c(np.asarray(L0_attn_norm_w, np.float32).reshape(16, 128).T)
    sg = c(np.asarray(L0_ffn_s_gate, np.float32).reshape(64, 128).T)
    su = c(np.asarray(L0_ffn_s_up, np.float32).reshape(64, 128).T)
    sd = c(np.asarray(L0_ffn_s_down, np.float32).reshape(1, 2048))
    p = np.arange(128, dtype=np.float32)
    qcolc = np.stack([-(p + 1), -(p + 129)], axis=1).astype(np.float32)
    kcolc = np.stack([p - 255, p - 127], axis=1).astype(np.float32)
    identc = np.eye(128, dtype=np.float32).astype(ml_dtypes.bfloat16)
    mm = np.arange(256, dtype=np.float32)
    mnv = np.concatenate([mm[None, :] - p[:, None],
                          mm[None, :] - 128.0 - p[:, None]], axis=1)
    masktc = (mnv >= 0).astype(np.float32)
    mncc = np.where(mnv >= 0, mnv, 1.0e4).astype(np.float32)
    shared = dict(wq=wq, wga=wga, wo=wo, wgf=wgf, wuf=wuf, wdf=wdf,
                  slope=slope, normw=normw, sgate=sg, sup=su, sdown=sd,
                  identc=identc, qcolc=qcolc, kcolc=kcolc, mncc=mncc,
                  masktc=masktc)
    in_maps = []
    for core in range(8):
        seq, half = core // 2, core % 2
        m = dict(shared)
        m["x_own"] = c(x[seq, half * T:(half + 1) * T, :])
        m["x_peer"] = c(x[seq, (1 - half) * T:(2 - half) * T, :])
        m["mask"] = np.array([[float(half)]], dtype=np.float32)
        in_maps.append(m)
    return in_maps


def kernel(**inputs):
    in_maps = _layout_inputs(**inputs)
    nc = build_nc()
    res = run_bass_kernel_spmd(nc, in_maps, core_ids=list(range(8)))
    out = np.empty((4, 2048, 2048), dtype=np.float32)
    for core in range(8):
        seq, half = core // 2, core % 2
        out[seq, half * T:(half + 1) * T, :] = res.results[core]["out"]
    return out
